# version 08 (final)
# speedup vs baseline: 1.0329x; 1.0110x over previous
import contextlib
import numpy as np

import concourse.bass as bass
import concourse.mybir as mybir
from concourse.bass_utils import run_bass_kernel_spmd

F32 = mybir.dt.float32
BF16 = mybir.dt.bfloat16
AF = mybir.ActivationFunctionType
ALU = mybir.AluOpType
AX = mybir.AxisListType

NCORES = 8
B, S, D = 4, 4096, 1024
TOK = S // 2


class Prog:
    def __init__(self):
        self.nc = bass.Bass("TRN2", target_bir_lowering=False)
        self.st = contextlib.ExitStack()
        self.sems = {}
        self.cnt = {}
        self.block = None
        self._scope = None
        self._pfx = ""

    def start_block(self):
        self.block = self.st.enter_context(self.nc.Block())

    def push_scope(self, pfx):
        assert self._scope is None
        self._scope = contextlib.ExitStack()
        self._pfx = pfx

    def pop_scope(self):
        self._scope.close()
        self._scope = None
        self._pfx = ""

    def sem(self, *names):
        for name in names:
            if name in self.sems:
                continue
            self.sems[name] = self.st.enter_context(self.nc.semaphore(name))
            self.cnt[name] = 0

    def sb(self, name, shape, dt):
        stk = self._scope if self._scope is not None else self.st
        return stk.enter_context(self.nc.sbuf_tensor(self._pfx + name, shape, dt))

    def ps(self, name, shape, dt=F32):
        stk = self._scope if self._scope is not None else self.st
        return stk.enter_context(self.nc.psum_tensor(self._pfx + name, shape, dt))

    def inc(self, instr, name, k=1):
        instr.then_inc(self.sems[name], k)
        self.cnt[name] += k
        return self.cnt[name]

    def wait(self, eng, name, val=None):
        eng.wait_ge(self.sems[name], self.cnt[name] if val is None else val)

    def sec(self, engname):
        def deco(fn):
            getattr(self.block, engname)(fn)
        return deco

    def barrier(self, engs=("tensor", "vector", "scalar", "sync")):
        if "bar" not in self.sems:
            self.sem("bar")
        n0 = self.cnt["bar"]
        for e in engs:
            @self.sec(e)
            def _(eng):
                self.inc(eng.drain(), "bar")
        target = n0 + len(engs)
        self.cnt["bar"] = target
        for e in engs:
            @self.sec(e)
            def _(eng):
                eng.wait_ge(self.sems["bar"], target)

    def close(self):
        self.st.close()


def _prep_w(w):
    O, K = w.shape
    nO, nK = O // 128, K // 128
    return np.ascontiguousarray(
        np.asarray(w, np.float32).T.reshape(nK, 128, nO, 128).transpose(2, 1, 0, 3))


def _prep_s(s):
    s = np.asarray(s, np.float32)
    return np.ascontiguousarray(s.reshape(-1, 128).T)


def _shard_xT(x):
    out = []
    for b in range(B):
        for h in range(2):
            out.append(np.ascontiguousarray(
                np.asarray(x[b, h * TOK:(h + 1) * TOK, :], np.float32).T))
    return out


def _unshard_xT(parts):
    x = np.empty((B, S, D), np.float32)
    for b in range(B):
        for h in range(2):
            x[b, h * TOK:(h + 1) * TOK, :] = parts[b * 2 + h].T
    return x


def emit_slab_load(P, g, wbuf, col0, wdram, o, nK, ldname):
    dst = wbuf[:, col0:col0 + nK * 128]
    src = wdram[o].rearrange("p k c -> p (k c)")
    P.inc(g.dma_start(dst, src), ldname, 16)


def emit_layernorm(P, *, xsrc_dram, tcol0, tcols, xbuf, xhat, xh_col0, scratch,
                   pS1, pS2, pBC1, pBC2, ones_col, ones_row, stat, eps=1e-5,
                   xh_w=None):
    nD = 8
    XHW = xh_w if xh_w is not None else xhat.shape[1] // nD

    @P.sec("sync")
    def _(sy):
        P.wait(sy, "lnfree")
        for k in range(nD):
            P.inc(sy.dma_start(xbuf[:, k * tcols:(k + 1) * tcols],
                               xsrc_dram[k * 128:(k + 1) * 128, tcol0:tcol0 + tcols]),
                  "ldx", 16)

    for rnd in range(2):
        @P.sec("scalar")
        def _(sc):
            P.wait(sc, "ldx")
            if rnd == 1:
                P.wait(sc, "sP")
            for j in range(4):
                k = rnd * 4 + j
                ins = sc.activation(scratch[:, j * tcols:(j + 1) * tcols],
                                    xbuf[:, k * tcols:(k + 1) * tcols], AF.Square)
            P.inc(ins, "sA")

        @P.sec("tensor")
        def _(t):
            P.wait(t, "ldx")
            for j in range(4):
                k = rnd * 4 + j
                t.matmul(pS1[0:1, 0:tcols], ones_col[:, 0:1],
                         xbuf[:, k * tcols:(k + 1) * tcols],
                         start=(k == 0), stop=(k == nD - 1))
            P.wait(t, "sA")
            for j in range(4):
                k = rnd * 4 + j
                ins = t.matmul(pS2[0:1, 0:tcols], ones_col[:, 0:1],
                               scratch[:, j * tcols:(j + 1) * tcols],
                               start=(k == 0), stop=(k == nD - 1))
            P.inc(ins, "sP")

    mean = stat[0:1, 0:tcols]
    var = stat[0:1, tcols:2 * tcols]
    rstd = stat[0:1, 2 * tcols:3 * tcols]
    nmr = stat[0:1, 3 * tcols:4 * tcols]

    @P.sec("vector")
    def _(v):
        P.wait(v, "sP")
        v.tensor_scalar_mul(mean, pS1[0:1, 0:tcols], 1.0 / D)
        P.inc(v.tensor_scalar_mul(var, pS2[0:1, 0:tcols], 1.0 / D), "sV")
        P.wait(v, "sV")
        P.inc(v.tensor_tensor(scratch[0:1, 0:tcols], mean, mean, ALU.mult), "sV")
        P.wait(v, "sV")
        P.inc(v.tensor_tensor(var, var, scratch[0:1, 0:tcols], ALU.subtract), "sV")
        P.wait(v, "sV")
        P.inc(v.tensor_scalar_add(var, var, float(eps)), "sV")

    @P.sec("scalar")
    def _(sc):
        P.wait(sc, "sV")
        P.inc(sc.activation(scratch[0:1, 0:tcols], var, AF.Sqrt), "sA")

    @P.sec("vector")
    def _(v):
        P.wait(v, "sA")
        P.inc(v.reciprocal(rstd, scratch[0:1, 0:tcols]), "sV")
        P.wait(v, "sV")
        P.inc(v.tensor_tensor(nmr, mean, rstd, ALU.mult), "sV")
        P.wait(v, "sV")
        P.inc(v.tensor_scalar_mul(nmr, nmr, -1.0), "sV")

    @P.sec("tensor")
    def _(t):
        P.wait(t, "sV")
        t.matmul(pBC1[:, 0:tcols], ones_row[0:1, 0:128], rstd, start=True, stop=True)
        ins = t.matmul(pBC2[:, 0:tcols], ones_row[0:1, 0:128], nmr, start=True, stop=True)
        P.inc(ins, "sP")

    @P.sec("vector")
    def _(v):
        P.wait(v, "sP")
        for k in range(nD):
            slot = scratch[:, (k % 2) * tcols:(k % 2) * tcols + tcols]
            P.inc(v.tensor_tensor(slot, xbuf[:, k * tcols:(k + 1) * tcols],
                                  pBC1[:, 0:tcols], ALU.mult), "sV")
            P.wait(v, "sV")
            ins = v.tensor_tensor(xhat[:, k * XHW + xh_col0:k * XHW + xh_col0 + tcols],
                                  slot, pBC2[:, 0:tcols], ALU.add)
            P.inc(ins, "sV")
            if k < nD - 1:
                P.wait(v, "sV")
        P.inc(v.drain(), "lnfree")


def emit_mm_stage(P, wbufs, banks, wdrams, nO, nK, rhs_buf, rhs_w, tiles,
                  epilogue, epi_cost, out_rows=128):
    nW = len(wdrams)
    slabw = nK * 128
    nTT = len(tiles)
    sp0 = P.cnt["sP"]
    sv0 = P.cnt["sV"]
    ldw0 = [P.cnt["ldwA"], P.cnt["ldwB"]]

    @P.sec("gpsimd")
    def _(g):
        for o in range(nO):
            if o < 2:
                g.wait_ge(P.sems["sP"], sp0)
            else:
                g.wait_ge(P.sems["sP"], sp0 + (o - 1) * nTT)
            wb = wbufs[o % 2]
            ldname = "ldwA" if o % 2 == 0 else "ldwB"
            for wd_i, wd in enumerate(wdrams):
                emit_slab_load(P, g, wb, wd_i * slabw, wd, o, nK, ldname)

    @P.sec("tensor")
    def _(t):
        for o in range(nO):
            ldname = "ldwA" if o % 2 == 0 else "ldwB"
            t.wait_ge(P.sems[ldname], ldw0[o % 2] + (o // 2 + 1) * nW * 16)
            wb = wbufs[o % 2]
            for ti, (col0, width) in enumerate(tiles):
                i = o * nTT + ti
                pair = banks[i % 2]
                if i >= 2:
                    t.wait_ge(P.sems["sV"], sv0 + (i - 1) * epi_cost)
                for wi in range(nW):
                    for k in range(nK):
                        ins = t.matmul(
                            pair[wi][0:out_rows, 0:width],
                            wb[:, wi * slabw + k * 128:wi * slabw + (k + 1) * 128][:, 0:out_rows],
                            rhs_buf[:, k * rhs_w + col0:k * rhs_w + col0 + width],
                            start=(k == 0), stop=(k == nK - 1))
                P.inc(ins, "sP")

    for o in range(nO):
        for ti, (col0, width) in enumerate(tiles):
            i = o * nTT + ti
            pair = banks[i % 2]
            epilogue(o, ti, (col0, width), pair, sp0 + i + 1, i, sv0)
    assert P.cnt["sV"] == sv0 + nO * nTT * epi_cost, \
        (P.cnt["sV"], sv0, nO, nTT, epi_cost)


def _io_ffn(nc, pfx="", internal=False, act_internal=False):
    return {
        "xT": nc.dram_tensor(pfx + "xT", [D, TOK], F32, kind="ExternalInput" if not act_internal else "Internal"),
        "y": nc.dram_tensor(pfx + "y", [D, TOK], F32, kind="ExternalOutput" if not act_internal else "Internal"),
        "w1p": nc.dram_tensor(pfx + "w1p", [32, 128, 8, 128], F32, kind="ExternalInput" if not internal else "Internal"),
        "w2p": nc.dram_tensor(pfx + "w2p", [32, 128, 8, 128], F32, kind="ExternalInput" if not internal else "Internal"),
        "h1p": nc.dram_tensor(pfx + "h1p", [32, 128, 32, 128], F32, kind="ExternalInput" if not internal else "Internal"),
        "h2p": nc.dram_tensor(pfx + "h2p", [32, 128, 32, 128], F32, kind="ExternalInput" if not internal else "Internal"),
        "w3p": nc.dram_tensor(pfx + "w3p", [8, 128, 32, 128], F32, kind="ExternalInput" if not internal else "Internal"),
        "s1": nc.dram_tensor(pfx + "s1", [128, 32], F32, kind="ExternalInput" if not internal else "Internal"),
        "s2": nc.dram_tensor(pfx + "s2", [128, 32], F32, kind="ExternalInput" if not internal else "Internal"),
        "sh1": nc.dram_tensor(pfx + "sh1", [128, 32], F32, kind="ExternalInput" if not internal else "Internal"),
        "sh2": nc.dram_tensor(pfx + "sh2", [128, 32], F32, kind="ExternalInput" if not internal else "Internal"),
        "s3": nc.dram_tensor(pfx + "s3", [128, 8], F32, kind="ExternalInput" if not internal else "Internal"),
    }


def emit_ffn(P, io, pfx=""):
    nc = P.nc
    P.push_scope(pfx)
    xT = io["xT"]
    y = io["y"]
    w1p = io["w1p"]
    w2p = io["w2p"]
    h1p = io["h1p"]
    h2p = io["h2p"]
    w3p = io["w3p"]
    s1d = io["s1"]
    s2d = io["s2"]
    sh1d = io["sh1"]
    sh2d = io["sh2"]
    s3d = io["s3"]

    TOKP = 1024
    TTW = 512
    nTT = TOKP // TTW

    xbuf = P.sb("xbuf", [128, 8 * TTW], F32)
    xhat = P.sb("xhat", [128, 8 * TOKP], BF16)
    hbuf = P.sb("hbuf", [128, 32 * TOKP], BF16)
    g1buf = P.sb("g1buf", [128, 32 * TOKP], BF16)
    scratch = P.sb("scratch", [128, 2048], F32)
    wbufA = P.sb("wbufA", [128, 4096], BF16)
    wbufB = P.sb("wbufB", [128, 4096], BF16)
    sS = P.sb("sS", [128, 32 * 4 + 8], F32)
    ones_col = P.sb("ones_col", [128, 1], F32)
    ones_row = P.sb("ones_row", [1, 128], F32)
    stat = P.sb("stat", [1, 4 * TTW], F32)
    xres = P.sb("xres", [128, 2 * TTW], F32)
    yout = P.sb("yout", [128, 2 * TTW], F32)

    pA = P.ps("pA", [128, 512])
    pB = P.ps("pB", [128, 512])
    pC = P.ps("pC", [128, 512])
    pD = P.ps("pD", [128, 512])
    pS1 = P.ps("pS1", [1, 512])
    pS2 = P.ps("pS2", [1, 512])

    P.sem("ldx", "ldwA", "ldwB", "ldrA", "ldrB", "sP", "sV", "sA",
          "stA", "stB", "lnfree")

    s1c = {"s1": (s1d, 0), "s2": (s2d, 32), "sh1": (sh1d, 64),
           "sh2": (sh2d, 96), "s3": (s3d, 128)}

    @P.sec("sync")
    def _(sy):
        for name, (dten, c0) in s1c.items():
            n = dten.shape[1]
            P.inc(sy.dma_start(sS[:, c0:c0 + n], dten[:, :]), "ldx", 16)

    @P.sec("vector")
    def _(v):
        v.memset(ones_col[:, :], 1.0)
        P.inc(v.memset(ones_row[:, :], 1.0), "lnfree")

    def scol(name, o):
        return sS[:, s1c[name][1] + o:s1c[name][1] + o + 1]

    P.barrier()

    for p in range(TOK // TOKP):
        tbase = p * TOKP
        for tt in range(nTT):
            emit_layernorm(P, xsrc_dram=xT, tcol0=tbase + tt * TTW, tcols=TTW,
                           xbuf=xbuf, xhat=xhat, xh_col0=tt * TTW, scratch=scratch,
                           pS1=pS1, pS2=pS2, pBC1=pA, pBC2=pB,
                           ones_col=ones_col, ones_row=ones_row, stat=stat)
        P.barrier()

        tiles = [(tt * TTW, TTW) for tt in range(nTT)]

        def stage(wdrams, nO, nK, rhs_buf, rhs_w, epilogue, epi_cost):
            emit_mm_stage(P, (wbufA, wbufB), [(pA, pB), (pC, pD)], wdrams,
                          nO, nK, rhs_buf, rhs_w, tiles, epilogue, epi_cost)

        def epi2(o, ti, tw, pair, spmark, i, sv0):
            col0, width = tw

            @P.sec("scalar")
            def _(sc):
                sc.wait_ge(P.sems["sP"], spmark)
                if i >= 2:
                    sc.wait_ge(P.sems["sV"], sv0 + (i - 1) * 2)
                P.inc(sc.activation(scratch[:, ti * TTW:ti * TTW + width],
                                    pair[0][:, 0:width], AF.Silu, scale=scol("s1", o)), "sA")

            @P.sec("vector")
            def _(v):
                P.wait(v, "sA")
                P.inc(v.tensor_scalar(scratch[:, 2 * TTW + ti * TTW:2 * TTW + ti * TTW + width],
                                      pair[1][:, 0:width], scol("s2", o), None, ALU.mult), "sV")
                P.wait(v, "sV")
                ins = v.tensor_tensor(
                    hbuf[:, o * TOKP + col0:o * TOKP + col0 + width],
                    scratch[:, ti * TTW:ti * TTW + width],
                    scratch[:, 2 * TTW + ti * TTW:2 * TTW + ti * TTW + width], ALU.mult)
                P.inc(ins, "sV")

        stage([w1p, w2p], 32, 8, xhat, TOKP, epi2, 2)
        P.barrier()

        def mk_epi_gelu(sname, dstbuf):
            def epi(o, ti, tw, pair, spmark, i, sv0):
                col0, width = tw

                @P.sec("scalar")
                def _(sc):
                    sc.wait_ge(P.sems["sP"], spmark)
                    P.inc(sc.activation(
                        dstbuf[:, o * TOKP + col0:o * TOKP + col0 + width],
                        pair[0][:, 0:width], AF.Gelu, scale=scol(sname, o)), "sA")

                @P.sec("vector")
                def _(v):
                    P.wait(v, "sA")
                    P.inc(v.nop(), "sV")
            return epi

        stage([h1p], 32, 32, hbuf, TOKP, mk_epi_gelu("sh1", g1buf), 1)
        P.barrier()

        stage([h2p], 32, 32, g1buf, TOKP, mk_epi_gelu("sh2", hbuf), 1)
        P.barrier()

        def epi5(o, ti, tw, pair, spmark, i, sv0):
            col0, width = tw
            slot = i % 2
            stname = "stA" if slot == 0 else "stB"
            ldrname = "ldrA" if slot == 0 else "ldrB"
            st_slot_prev = P.cnt[stname]

            @P.sec("sync")
            def _(sy):
                if st_slot_prev > 0:
                    sy.wait_ge(P.sems[stname], st_slot_prev)
                P.inc(sy.dma_start(xres[:, slot * TTW:slot * TTW + width],
                                   xT[o * 128:(o + 1) * 128,
                                      tbase + col0:tbase + col0 + width]),
                      ldrname, 16)

            @P.sec("vector")
            def _(v):
                v.wait_ge(P.sems["sP"], spmark)
                P.wait(v, ldrname)
                if st_slot_prev > 0:
                    v.wait_ge(P.sems[stname], st_slot_prev)
                ins = v.scalar_tensor_tensor(
                    yout[:, slot * TTW:slot * TTW + width], pair[0][:, 0:width],
                    scol("s3", o), xres[:, slot * TTW:slot * TTW + width],
                    ALU.mult, ALU.add)
                P.inc(ins, "sV")

            @P.sec("sync")
            def _(sy):
                P.wait(sy, "sV")
                P.inc(sy.dma_start(y[o * 128:(o + 1) * 128,
                                     tbase + col0:tbase + col0 + width],
                                   yout[:, slot * TTW:slot * TTW + width]), stname, 16)

        stage([w3p], 8, 32, hbuf, TOKP, epi5, 1)
        P.barrier()

    @P.sec("sync")
    def _(sy):
        P.wait(sy, "stA")
        P.wait(sy, "stB")

    P.pop_scope()



def build_ffn():
    P = Prog()
    io = _io_ffn(P.nc)
    P.start_block()
    emit_ffn(P, io)
    P.close()
    return P.nc


def _io_merge(nc, pfx="", internal=False, act_internal=False):
    return {
        "x0": nc.dram_tensor(pfx + "x0", [D, TOK], F32, kind="ExternalInput" if not act_internal else "Internal"),
        "x1": nc.dram_tensor(pfx + "x1", [D, TOK], F32, kind="ExternalInput" if not act_internal else "Internal"),
        "x2": nc.dram_tensor(pfx + "x2", [D, TOK], F32, kind="ExternalInput" if not act_internal else "Internal"),
        "bd": nc.dram_tensor(pfx + "bd", [128, 8], F32, kind="ExternalInput" if not internal else "Internal"),
        "y": nc.dram_tensor(pfx + "y", [D, TOK], F32, kind="ExternalOutput" if not act_internal else "Internal"),
    }


def emit_merge(P, io, pfx=""):
    nc = P.nc
    P.push_scope(pfx)
    x0 = io["x0"]
    x1 = io["x1"]
    x2 = io["x2"]
    bd = io["bd"]
    y = io["y"]
    a = P.sb("a", [128, TOK], F32)
    b = P.sb("b", [128, TOK], F32)
    c = P.sb("c", [128, TOK], F32)
    bb = P.sb("bb", [128, 8], F32)
    P.sem("ld", "sV", "st")

    @P.sec("sync")
    def _(sy):
        P.inc(sy.dma_start(bb[:, :], bd[:, :]), "ld", 16)
    st0 = P.cnt["st"]
    for k in range(8):
        @P.sec("sync")
        def _(sy):
            if k > 0:
                P.wait(sy, "sV")
                sy.wait_ge(P.sems["st"], st0 + 16 * k)
            P.inc(sy.dma_start(a[:, :], x0[k * 128:(k + 1) * 128, :]), "ld", 16)
            P.inc(sy.dma_start(b[:, :], x1[k * 128:(k + 1) * 128, :]), "ld", 16)
            P.inc(sy.dma_start(c[:, :], x2[k * 128:(k + 1) * 128, :]), "ld", 16)

        @P.sec("vector")
        def _(v):
            P.wait(v, "ld")
            if k > 0:
                P.wait(v, "st")
            P.inc(v.tensor_tensor(a[:, :], a[:, :], b[:, :], ALU.add), "sV")
            P.wait(v, "sV")
            P.inc(v.tensor_tensor(a[:, :], a[:, :], c[:, :], ALU.add), "sV")
            P.wait(v, "sV")
            P.inc(v.tensor_scalar(a[:, :], a[:, :], bb[:, k:k + 1], None,
                                  ALU.add), "sV")

        @P.sec("sync")
        def _(sy):
            P.wait(sy, "sV")
            P.inc(sy.dma_start(y[k * 128:(k + 1) * 128, :], a[:, :]), "st", 16)

    @P.sec("sync")
    def _(sy):
        P.wait(sy, "st")
    P.pop_scope()



def build_merge():
    P = Prog()
    io = _io_merge(P.nc)
    P.start_block()
    emit_merge(P, io)
    P.close()
    return P.nc


def _io_attn_b(nc, pfx="", internal=False, act_internal=False):
    return {
        "xTe": nc.dram_tensor(pfx + "xTe", [D, TOK + 128], F32, kind="ExternalInput" if not act_internal else "Internal"),
        "y": nc.dram_tensor(pfx + "y", [D, TOK], F32, kind="ExternalOutput" if not act_internal else "Internal"),
        "wqp": nc.dram_tensor(pfx + "wqp", [8, 128, 8, 128], F32, kind="ExternalInput" if not internal else "Internal"),
        "wkp": nc.dram_tensor(pfx + "wkp", [8, 128, 8, 128], F32, kind="ExternalInput" if not internal else "Internal"),
        "wvT": nc.dram_tensor(pfx + "wvT", [D, D], F32, kind="ExternalInput" if not internal else "Internal"),
        "wop": nc.dram_tensor(pfx + "wop", [8, 128, 8, 128], F32, kind="ExternalInput" if not internal else "Internal"),
        "mrest": nc.dram_tensor(pfx + "mrest", [128, 256], F32, kind="ExternalInput" if not internal else "Internal"),
        "mfirst": nc.dram_tensor(pfx + "mfirst", [128, 256], F32, kind="ExternalInput" if not internal else "Internal"),
    }


def emit_attn_b(P, io, pfx=""):
    nc = P.nc
    P.push_scope(pfx)
    TE = TOK + 128
    xTe = io["xTe"]
    y = io["y"]
    wqp = io["wqp"]
    wkp = io["wkp"]
    wvT = io["wvT"]
    wop = io["wop"]
    mrest = io["mrest"]
    mfirst = io["mfirst"]

    TTW = 512
    ln_tiles = [(i * 512, 512) for i in range(4)] + [(2048, 128)]
    q_tiles = [(128 + i * 512, 512) for i in range(4)]

    xbuf = P.sb("xbuf", [128, 8 * TTW], F32)
    xhat = P.sb("xhat", [128, 8 * TE], BF16)
    qbuf = P.sb("qbuf", [128, 8 * TOK], BF16)
    kbuf = P.sb("kbuf", [128, 8 * TE], BF16)
    vt = P.sb("vt", [128, 17 * 1024], BF16)
    wvs = P.sb("wvs", [128, 8 * 1024], BF16)
    scratch = P.sb("scratch", [128, 2048], F32)
    wbufA = P.sb("wbufA", [128, 1024], BF16)
    wbufB = P.sb("wbufB", [128, 1024], BF16)
    ones_col = P.sb("ones_col", [128, 1], F32)
    ones_row = P.sb("ones_row", [1, 128], F32)
    idbf = P.sb("idbf", [128, 128], BF16)
    stat = P.sb("stat", [1, 4 * TTW], F32)
    masks = P.sb("masks", [128, 512], F32)
    smask = P.sb("smask", [128, 2 * 256], F32)
    ptile = P.sb("ptile", [128, 2 * 256], BF16)
    ptb = P.sb("ptb", [128, 2 * 256], BF16)
    avtile = P.sb("avtile", [64, 2 * 128], BF16)
    rowst = P.sb("rowst", [128, 8], F32)
    xres = P.sb("xres", [128, 2 * TTW], F32)
    yout = P.sb("yout", [128, 2 * TTW], F32)

    pA = P.ps("pA", [128, 512])
    pB = P.ps("pB", [128, 512])
    pC = P.ps("pC", [128, 512])
    pD = P.ps("pD", [128, 512])
    pS1 = P.ps("pS1", [1, 512])
    pS2 = P.ps("pS2", [1, 512])
    pT1 = P.ps("pT1", [128, 256], BF16)
    pT2 = P.ps("pT2", [128, 256], BF16)

    P.sem("ldx", "ldc", "ldwA", "ldwB", "ldrA", "ldrB", "sP", "sV", "sA",
          "stA", "stB", "lnfree", "ldv")

    id_dram = nc.inline_tensor(np.eye(128, dtype=np.float16).astype(
        np.float32), name=pfx + "idf32")

    @P.sec("gpsimd")
    def _(g):
        for k in range(8):
            P.inc(g.dma_start(wvs[:, k * 1024:(k + 1) * 1024],
                              wvT[k * 128:(k + 1) * 128, :]), "ldv", 16)
        P.inc(g.dma_start(idbf[:, :], id_dram[:, :]), "ldv", 16)

    @P.sec("sync")
    def _(sy):
        P.inc(sy.dma_start(masks[:, 0:256], mrest[:, :]), "ldc", 16)
        P.inc(sy.dma_start(masks[:, 256:512], mfirst[:, :]), "ldc", 16)

    @P.sec("vector")
    def _(v):
        v.memset(ones_col[:, :], 1.0)
        P.inc(v.memset(ones_row[:, :], 1.0), "lnfree")

    P.barrier()

    for ti, (c0, w) in enumerate(ln_tiles):
        emit_layernorm(P, xsrc_dram=xTe, tcol0=c0, tcols=w,
                       xbuf=xbuf, xhat=xhat, xh_col0=c0, scratch=scratch,
                       pS1=pS1, pS2=pS2, pBC1=pA, pBC2=pB,
                       ones_col=ones_col, ones_row=ones_row, stat=stat)
    P.barrier()

    def mk_epi_qk(dstbuf, dst_w, off, scale):
        def epi(o, ti, tw, pair, spmark, i, sv0):
            col0, width = tw

            @P.sec("scalar")
            def _(sc):
                sc.wait_ge(P.sems["sP"], spmark)
                P.inc(sc.activation(
                    dstbuf[:, o * dst_w + col0 - off:o * dst_w + col0 - off + width],
                    pair[0][:, 0:width], AF.Copy, scale=float(scale)), "sA")

            @P.sec("vector")
            def _(v):
                P.wait(v, "sA")
                P.inc(v.nop(), "sV")
        return epi

    emit_mm_stage(P, (wbufA, wbufB), [(pA, pB), (pC, pD)], [wqp], 8, 8,
                  xhat, TE, q_tiles, mk_epi_qk(qbuf, TOK, 128, 1.0 / 0.56), 1)
    P.barrier()
    emit_mm_stage(P, (wbufA, wbufB), [(pA, pB), (pC, pD)], [wkp], 8, 8,
                  xhat, TE, ln_tiles, mk_epi_qk(kbuf, TE, 0, 1.0), 1)
    P.barrier()

    sp0 = P.cnt["sP"]
    sa0 = P.cnt["sA"]
    vbanks = [pA, pB, pC, pD]

    @P.sec("tensor")
    def _(t):
        P.wait(t, "ldv")
        for e in range(17):
            for c in range(2):
                i = e * 2 + c
                if i >= 4:
                    t.wait_ge(P.sems["sA"], sa0 + i - 3)
                for k in range(8):
                    ins = t.matmul(vbanks[i % 4][:, 0:512],
                                   xhat[:, k * TE + e * 128:k * TE + (e + 1) * 128],
                                   wvs[:, k * 1024 + c * 512:k * 1024 + (c + 1) * 512],
                                   start=(k == 0), stop=(k == 7))
                P.inc(ins, "sP")

    for e in range(17):
        for c in range(2):
            i = e * 2 + c

            @P.sec("scalar")
            def _(sc):
                sc.wait_ge(P.sems["sP"], sp0 + i + 1)
                P.inc(sc.activation(vt[:, e * 1024 + c * 512:e * 1024 + (c + 1) * 512],
                                    vbanks[i % 4][:, 0:512], AF.Copy), "sA")
    P.barrier()

    attnout = xhat
    sp0 = P.cnt["sP"]
    sv0 = P.cnt["sV"]
    sa0 = P.cnt["sA"]
    NH, NQB = 16, 16
    for h in range(NH):
        for qb in range(NQB):
            i = h * NQB + qb
            slot = i % 2
            pS = pA if slot == 0 else pB
            pAV = pC if slot == 0 else pD
            pT = pT1 if slot == 0 else pT2
            hrow = (h % 2) * 64
            kc = h // 2

            @P.sec("tensor")
            def _(t):
                if i >= 2:
                    t.wait_ge(P.sems["sV"], sv0 + (i - 2) * 5 + 1)
                ins = t.matmul(pS[:, 0:256],
                               qbuf[hrow:hrow + 64, kc * TOK + qb * 128:kc * TOK + (qb + 1) * 128],
                               kbuf[hrow:hrow + 64, kc * TE + qb * 128:kc * TE + (qb + 2) * 128],
                               start=True, stop=True)
                P.inc(ins, "sP")

            @P.sec("vector")
            def _(v):
                v.wait_ge(P.sems["sP"], sp0 + i * 3 + 1)
                if i < 2:
                    P.wait(v, "ldc")
                if i >= 2:
                    v.wait_ge(P.sems["sA"], sa0 + (i - 2) * 2 + 1)
                m0 = 256 if qb == 0 else 0
                P.inc(v.tensor_tensor(smask[:, slot * 256:slot * 256 + 256],
                                      pS[:, 0:256], masks[:, m0:m0 + 256], ALU.add),
                      "sV")
                P.wait(v, "sV")
                P.inc(v.reduce_max(rowst[:, slot:slot + 1],
                                   smask[:, slot * 256:slot * 256 + 256],
                                   axis=AX.X, negate=True), "sV")

            @P.sec("scalar")
            def _(sc):
                P.wait(sc, "sV")
                if i >= 2:
                    sc.wait_ge(P.sems["sP"], sp0 + (i - 2) * 3 + 2)
                P.inc(sc.activation(ptile[:, slot * 256:slot * 256 + 256],
                                    smask[:, slot * 256:slot * 256 + 256], AF.Exp,
                                    bias=rowst[:, slot:slot + 1],
                                    accum_out=rowst[:, 4 + slot:5 + slot]), "sA")

            @P.sec("vector")
            def _(v):
                P.wait(v, "sA")
                P.inc(v.reciprocal(rowst[:, 2 + slot:3 + slot],
                                   rowst[:, 4 + slot:5 + slot]), "sV")
                P.wait(v, "sV")
                P.inc(v.tensor_scalar(ptile[:, slot * 256:slot * 256 + 256],
                                      ptile[:, slot * 256:slot * 256 + 256],
                                      rowst[:, 2 + slot:3 + slot], None, ALU.mult),
                      "sV")

            @P.sec("tensor")
            def _(t):
                P.wait(t, "sV")
                t.transpose(pT[:, 0:128], ptile[:, slot * 256:slot * 256 + 128],
                            idbf[:, :])
                ins = t.transpose(pT[:, 128:256],
                                  ptile[:, slot * 256 + 128:slot * 256 + 256],
                                  idbf[:, :])
                P.inc(ins, "sP")

            @P.sec("vector")
            def _(v):
                v.wait_ge(P.sems["sP"], sp0 + i * 3 + 2)
                P.inc(v.tensor_copy(ptb[:, slot * 256:slot * 256 + 256],
                                    pT[:, 0:256]), "sV")

            @P.sec("tensor")
            def _(t):
                P.wait(t, "sV")
                if i >= 2:
                    t.wait_ge(P.sems["sA"], sa0 + (i - 2) * 2 + 2)
                t.matmul(pAV[0:64, 0:128],
                         vt[:, qb * 1024 + h * 64:qb * 1024 + h * 64 + 64],
                         ptb[:, slot * 256:slot * 256 + 128],
                         start=True, stop=False)
                ins = t.matmul(pAV[0:64, 0:128],
                               vt[:, (qb + 1) * 1024 + h * 64:(qb + 1) * 1024 + h * 64 + 64],
                               ptb[:, slot * 256 + 128:slot * 256 + 256],
                               start=False, stop=True)
                P.inc(ins, "sP")

            @P.sec("vector")
            def _(v):
                v.wait_ge(P.sems["sP"], sp0 + i * 3 + 3)
                P.inc(v.tensor_copy(
                    attnout[hrow:hrow + 64, kc * TE + qb * 128:kc * TE + (qb + 1) * 128],
                    pAV[0:64, 0:128]), "sA")

    P.barrier()

    def epi_out(o, ti, tw, pair, spmark, i, sv0):
        col0, width = tw
        slot = i % 2
        stname = "stA" if slot == 0 else "stB"
        ldrname = "ldrA" if slot == 0 else "ldrB"
        st_slot_prev = P.cnt[stname]

        @P.sec("sync")
        def _(sy):
            if st_slot_prev > 0:
                sy.wait_ge(P.sems[stname], st_slot_prev)
            P.inc(sy.dma_start(xres[:, slot * TTW:slot * TTW + width],
                               xTe[o * 128:(o + 1) * 128, 128 + col0:128 + col0 + width]),
                  ldrname, 16)

        @P.sec("vector")
        def _(v):
            v.wait_ge(P.sems["sP"], spmark)
            P.wait(v, ldrname)
            if st_slot_prev > 0:
                v.wait_ge(P.sems[stname], st_slot_prev)
            ins = v.scalar_tensor_tensor(
                yout[:, slot * TTW:slot * TTW + width], pair[0][:, 0:width],
                1.0, xres[:, slot * TTW:slot * TTW + width], ALU.mult, ALU.add)
            P.inc(ins, "sV")

        @P.sec("sync")
        def _(sy):
            P.wait(sy, "sV")
            P.inc(sy.dma_start(y[o * 128:(o + 1) * 128, col0:col0 + width],
                               yout[:, slot * TTW:slot * TTW + width]), stname, 16)

    emit_mm_stage(P, (wbufA, wbufB), [(pA, pB), (pC, pD)], [wop], 8, 8,
                  attnout, TE, [(i * 512, 512) for i in range(4)], epi_out, 1)

    @P.sec("sync")
    def _(sy):
        P.wait(sy, "stA")
        P.wait(sy, "stB")
    P.pop_scope()



def build_attn_b():
    P = Prog()
    io = _io_attn_b(P.nc)
    P.start_block()
    emit_attn_b(P, io)
    P.close()
    return P.nc


def _io_attn_a(nc, pfx="", internal=False, act_internal=False):
    return {
        "xTf": nc.dram_tensor(pfx + "xTf", [D, S], F32, kind="ExternalInput" if not act_internal else "Internal"),
        "y": nc.dram_tensor(pfx + "y", [D, S], F32, kind="ExternalOutput" if not act_internal else "Internal"),
        "wqp": nc.dram_tensor(pfx + "wqp", [4, 128, 8, 128], F32, kind="ExternalInput" if not internal else "Internal"),
        "wqpp": nc.dram_tensor(pfx + "wqpp", [4, 128, 8, 128], F32, kind="ExternalInput" if not internal else "Internal"),
        "wkp": nc.dram_tensor(pfx + "wkp", [4, 128, 8, 128], F32, kind="ExternalInput" if not internal else "Internal"),
        "wkpp": nc.dram_tensor(pfx + "wkpp", [4, 128, 8, 128], F32, kind="ExternalInput" if not internal else "Internal"),
        "wvT": nc.dram_tensor(pfx + "wvT", [D, 512], F32, kind="ExternalInput" if not internal else "Internal"),
        "wop": nc.dram_tensor(pfx + "wop", [8, 128, 4, 128], F32, kind="ExternalInput" if not internal else "Internal"),
        "bq": nc.dram_tensor(pfx + "bq", [128, 4], F32, kind="ExternalInput" if not internal else "Internal"),
        "bqp": nc.dram_tensor(pfx + "bqp", [128, 4], F32, kind="ExternalInput" if not internal else "Internal"),
        "bk": nc.dram_tensor(pfx + "bk", [128, 4], F32, kind="ExternalInput" if not internal else "Internal"),
        "bkp": nc.dram_tensor(pfx + "bkp", [128, 4], F32, kind="ExternalInput" if not internal else "Internal"),
        "bv": nc.dram_tensor(pfx + "bv", [1, 512], F32, kind="ExternalInput" if not internal else "Internal"),
        "cosd": nc.dram_tensor(pfx + "cosd", [128, S], F32, kind="ExternalInput" if not internal else "Internal"),
        "sind": nc.dram_tensor(pfx + "sind", [128, S], F32, kind="ExternalInput" if not internal else "Internal"),
    }


def emit_attn_a(P, io, pfx=""):
    nc = P.nc
    P.push_scope(pfx)
    TF = S
    xTf = io["xTf"]
    y = io["y"]
    wqp = io["wqp"]
    wqpp = io["wqpp"]
    wkp = io["wkp"]
    wkpp = io["wkpp"]
    wvT = io["wvT"]
    wop = io["wop"]
    bqd = io["bq"]
    bqpd = io["bqp"]
    bkd = io["bk"]
    bkpd = io["bkp"]
    bvd = io["bv"]
    cosd = io["cosd"]
    sind = io["sind"]

    TTW = 512
    nTile = TF // TTW

    LNW = 256
    HT = TF // 2
    xbuf = P.sb("xbuf", [128, 8 * LNW], F32)
    xhat = P.sb("xhat", [128, 8 * HT], BF16)
    qbuf = P.sb("qbuf", [128, 4 * TF], BF16)
    kbuf = P.sb("kbuf", [128, 4 * TF], BF16)
    vt = P.sb("vt", [128, 32 * 512], BF16)
    wvs = P.sb("wvs", [128, 8 * 512], BF16)
    cosf = P.sb("cosf", [128, TF], BF16)
    sinf = P.sb("sinf", [128, TF], BF16)
    scratch = P.sb("scratch", [128, 1024], F32)
    wbufA = P.sb("wbufA", [128, 2048], BF16)
    wbufB = P.sb("wbufB", [128, 2048], BF16)
    biasb = P.sb("biasb", [128, 16], F32)
    bvb = P.sb("bvb", [1, 512], F32)
    ones_col = P.sb("ones_col", [128, 1], F32)
    ones_bf = P.sb("ones_bf", [128, 1], BF16)
    ones_row = P.sb("ones_row", [1, 512], F32)
    shiftc = P.sb("shiftc", [128, 1], F32)
    ptile = P.sb("ptile", [128, 2 * 1024], BF16)
    avt = P.sb("avt", [64, 1024], BF16)
    rrow = P.sb("rrow", [1, 1024], F32)
    yout = P.sb("yout", [128, 2 * TTW], F32)
    stat = P.sb("stat", [1, 4 * LNW], F32)

    pstk1 = contextlib.ExitStack()
    pA = pstk1.enter_context(nc.psum_tensor(pfx + "pA", [128, 512], F32))
    pB = pstk1.enter_context(nc.psum_tensor(pfx + "pB", [128, 512], F32))
    pC = pstk1.enter_context(nc.psum_tensor(pfx + "pC", [128, 512], F32))
    pD = pstk1.enter_context(nc.psum_tensor(pfx + "pD", [128, 512], F32))
    pS1 = pstk1.enter_context(nc.psum_tensor(pfx + "pS1", [1, 512], F32))
    pS2 = pstk1.enter_context(nc.psum_tensor(pfx + "pS2", [1, 512], F32))

    P.sem("ldx", "ldc", "ldt", "ldwA", "ldwB", "sP", "sV", "sA", "stA", "stB",
          "lnfree", "ldv")

    @P.sec("gpsimd")
    def _(g):
        for k in range(8):
            P.inc(g.dma_start(wvs[:, k * 512:(k + 1) * 512],
                              wvT[k * 128:(k + 1) * 128, :]), "ldv", 16)

    @P.sec("gpsimd")
    def _(g):
        P.inc(g.dma_start(cosf[:, :], cosd[:, :]), "ldt", 16)
        P.inc(g.dma_start(sinf[:, :], sind[:, :]), "ldt", 16)

    @P.sec("sync")
    def _(sy):
        for i, dt_ in enumerate([bqd, bqpd, bkd, bkpd]):
            P.inc(sy.dma_start(biasb[:, i * 4:(i + 1) * 4], dt_[:, :]), "ldc", 16)
        P.inc(sy.dma_start(bvb[:, :], bvd[:, :]), "ldc", 16)

    @P.sec("vector")
    def _(v):
        v.memset(ones_col[:, :], 1.0)
        v.memset(ones_bf[:, :], 1.0)
        v.memset(ones_row[:, :], 1.0)
        v.memset(shiftc[:, :], -8.0)
        P.inc(v.drain(), "lnfree")

    P.barrier()

    def mk_epi_rope(dstbuf, bcol, bpcol, goff):
        def epi(o, ti, tw, pair, spmark, i, sv0):
            col0, width = tw

            @P.sec("vector")
            def _(v):
                v.wait_ge(P.sems["sP"], spmark)
                P.wait(v, "ldc", 5 * 16)
                P.wait(v, "ldt", 2 * 16)
                if i >= 1:
                    v.wait_ge(P.sems["sV"], sv0 + i * 3)
                P.inc(v.scalar_tensor_tensor(
                    scratch[:, 0:width], pair[0][:, 0:width],
                    biasb[:, bcol + o:bcol + o + 1],
                    cosf[:, goff + col0:goff + col0 + width],
                    ALU.add, ALU.mult), "sV")
                P.inc(v.scalar_tensor_tensor(
                    scratch[:, TTW:TTW + width], pair[1][:, 0:width],
                    biasb[:, bpcol + o:bpcol + o + 1],
                    sinf[:, goff + col0:goff + col0 + width],
                    ALU.add, ALU.mult), "sV")
                P.wait(v, "sV")
                P.inc(v.tensor_tensor(
                    dstbuf[:, o * TF + goff + col0:o * TF + goff + col0 + width],
                    scratch[:, 0:width], scratch[:, TTW:TTW + width], ALU.add), "sV")
        return epi

    tiles4 = [(i * TTW, TTW) for i in range(HT // TTW)]
    for h2 in range(2):
        goff = h2 * HT
        for it in range(HT // LNW):
            emit_layernorm(P, xsrc_dram=xTf, tcol0=goff + it * LNW, tcols=LNW,
                           xbuf=xbuf, xhat=xhat, xh_col0=it * LNW, scratch=scratch,
                           pS1=pS1, pS2=pS2, pBC1=pA, pBC2=pB,
                           ones_col=ones_col, ones_row=ones_row, stat=stat)
        P.barrier()
        emit_mm_stage(P, (wbufA, wbufB), [(pA, pB), (pC, pD)], [wqp, wqpp], 4, 8,
                      xhat, HT, tiles4, mk_epi_rope(qbuf, 0, 4, goff), 3)
        P.barrier()
        emit_mm_stage(P, (wbufA, wbufB), [(pA, pB), (pC, pD)], [wkp, wkpp], 4, 8,
                      xhat, HT, tiles4, mk_epi_rope(kbuf, 8, 12, goff), 3)
        P.barrier()

        sp0 = P.cnt["sP"]
        sa0 = P.cnt["sA"]
        vbanks = [pA, pB, pC, pD]

        @P.sec("tensor")
        def _(t):
            P.wait(t, "ldv")
            P.wait(t, "ldc", 5 * 16)
            for e in range(16):
                if e >= 4:
                    t.wait_ge(P.sems["sA"], sa0 + e - 3)
                for k in range(8):
                    t.matmul(vbanks[e % 4][:, 0:512],
                             xhat[:, k * HT + e * 128:k * HT + (e + 1) * 128],
                             wvs[:, k * 512:(k + 1) * 512],
                             start=(k == 0), stop=False)
                ins = t.matmul(vbanks[e % 4][:, 0:512], ones_row[0:1, 0:128],
                               bvb[0:1, :], start=False, stop=True)
                P.inc(ins, "sP")

        for e in range(16):
            @P.sec("scalar")
            def _(sc):
                sc.wait_ge(P.sems["sP"], sp0 + e + 1)
                P.inc(sc.activation(vt[:, (h2 * 16 + e) * 512:(h2 * 16 + e + 1) * 512],
                                    vbanks[e % 4][:, 0:512], AF.Copy), "sA")
        P.barrier()

    pstk1.close()
    pstk2 = contextlib.ExitStack()
    pE = pstk2.enter_context(nc.psum_tensor(pfx + "pE", [128, 2048], F32))
    pav1 = pstk2.enter_context(nc.psum_tensor(pfx + "pav1", [64, 512], F32))
    pav2 = pstk2.enter_context(nc.psum_tensor(pfx + "pav2", [64, 512], F32))
    prow1 = pstk2.enter_context(nc.psum_tensor(pfx + "prow1", [1, 512], F32))
    prow2 = pstk2.enter_context(nc.psum_tensor(pfx + "prow2", [1, 512], F32))
    attnout = xhat
    sv0 = P.cnt["sV"]
    NH = 8
    QW = 1024
    nQT = TF // QW
    NKJ = 32
    for h in range(NH):
        for qt in range(nQT):
            inst = h * nQT + qt
            hrow = (h % 2) * 64
            kc = h // 2
            pe0 = P.cnt["sP"]
            ac0 = P.cnt["sA"]

            def emit_S(t, kj):
                if kj < 2:
                    if inst >= 1:
                        t.wait_ge(P.sems["sV"], sv0 + inst * 3)
                else:
                    t.wait_ge(P.sems["sA"], ac0 + kj - 1)
                for half in range(2):
                    ins = t.matmul(
                        pE[:, (kj % 2) * 1024 + half * 512:
                           (kj % 2) * 1024 + (half + 1) * 512],
                        kbuf[hrow:hrow + 64, kc * TF + kj * 128:kc * TF + (kj + 1) * 128],
                        qbuf[hrow:hrow + 64, kc * TF + qt * QW + half * 512:
                             kc * TF + qt * QW + (half + 1) * 512],
                        start=True, stop=True)
                P.inc(ins, "sP")

            def emit_AV(t, kj):
                t.wait_ge(P.sems["sA"], ac0 + kj + 1)
                if kj == 0 and inst >= 1:
                    t.wait_ge(P.sems["sV"], sv0 + inst * 3)
                pslot = ptile[:, (kj % 2) * 1024:(kj % 2 + 1) * 1024]
                t.matmul(pav1[0:64, 0:512],
                         vt[:, kj * 512 + h * 64:kj * 512 + h * 64 + 64],
                         pslot[:, 0:512], start=(kj == 0), stop=(kj == NKJ - 1))
                t.matmul(pav2[0:64, 0:512],
                         vt[:, kj * 512 + h * 64:kj * 512 + h * 64 + 64],
                         pslot[:, 512:1024], start=(kj == 0), stop=(kj == NKJ - 1))
                t.matmul(prow1[0:1, 0:512], ones_bf[:, 0:1],
                         pslot[:, 0:512], start=(kj == 0), stop=(kj == NKJ - 1))
                ins = t.matmul(prow2[0:1, 0:512], ones_bf[:, 0:1],
                               pslot[:, 512:1024], start=(kj == 0),
                               stop=(kj == NKJ - 1))
                P.inc(ins, "sP")

            @P.sec("tensor")
            def _(t):
                for kj in range(NKJ):
                    emit_S(t, kj)
                    if kj >= 1:
                        emit_AV(t, kj - 1)
                emit_AV(t, NKJ - 1)

            def sp_S(kj):
                return pe0 + (1 if kj == 0 else 2 * kj)

            def sp_AV(kj):
                return pe0 + (64 if kj == NKJ - 1 else 2 * kj + 3)

            for kj in range(NKJ):
                @P.sec("scalar")
                def _(sc):
                    sc.wait_ge(P.sems["sP"], sp_S(kj))
                    if kj >= 2:
                        sc.wait_ge(P.sems["sP"], sp_AV(kj - 2))
                    P.inc(sc.activation(ptile[:, (kj % 2) * 1024:(kj % 2 + 1) * 1024],
                                        pE[:, (kj % 2) * 1024:(kj % 2 + 1) * 1024],
                                        AF.Exp, bias=shiftc[:, 0:1], scale=0.125),
                          "sA")

            @P.sec("vector")
            def _(v):
                v.wait_ge(P.sems["sP"], pe0 + 64)
                v.tensor_copy(rrow[0:1, 0:512], prow1[0:1, 0:512])
                v.tensor_copy(rrow[0:1, 512:1024], prow2[0:1, 0:512])
                P.inc(v.drain(), "sV")
                P.wait(v, "sV")
                P.inc(v.reciprocal(rrow[0:1, 0:1024], rrow[0:1, 0:1024]), "sV")

            @P.sec("tensor")
            def _(t):
                P.wait(t, "sV")
                t.matmul(pE[0:64, 0:512], ones_row[0:1, 0:64],
                         rrow[0:1, 0:512], start=True, stop=True)
                ins = t.matmul(pE[0:64, 512:1024], ones_row[0:1, 0:64],
                               rrow[0:1, 512:1024], start=True, stop=True)
                P.inc(ins, "sP")

            @P.sec("scalar")
            def _(sc):
                sc.wait_ge(P.sems["sP"], pe0 + 64)
                sc.activation(avt[:, 0:512], pav1[0:64, 0:512], AF.Copy)
                P.inc(sc.activation(avt[:, 512:1024], pav2[0:64, 0:512], AF.Copy),
                      "sA")

            @P.sec("vector")
            def _(v):
                P.wait(v, "sA")
                P.wait(v, "sP")
                ins = v.tensor_tensor(
                    attnout[hrow:hrow + 64, kc * TF + qt * QW:kc * TF + qt * QW + QW],
                    avt[:, 0:1024], pE[0:64, 0:1024], ALU.mult)
                P.inc(ins, "sV")

            assert P.cnt["sV"] == sv0 + (inst + 1) * 3, (P.cnt["sV"], sv0, inst)
    P.barrier()
    pstk2.close()
    pstk3 = contextlib.ExitStack()
    pA = pstk3.enter_context(nc.psum_tensor(pfx + "pA3", [128, 512], F32))
    pB = pstk3.enter_context(nc.psum_tensor(pfx + "pB3", [128, 512], F32))
    pC = pstk3.enter_context(nc.psum_tensor(pfx + "pC3", [128, 512], F32))
    pD = pstk3.enter_context(nc.psum_tensor(pfx + "pD3", [128, 512], F32))

    def epi_part(o, ti, tw, pair, spmark, i, sv0_):
        col0, width = tw
        slot = i % 2
        stname = "stA" if slot == 0 else "stB"
        st_slot_prev = P.cnt[stname]

        @P.sec("vector")
        def _(v):
            v.wait_ge(P.sems["sP"], spmark)
            if st_slot_prev > 0:
                v.wait_ge(P.sems[stname], st_slot_prev)
            P.inc(v.tensor_copy(yout[:, slot * TTW:slot * TTW + width],
                                pair[0][:, 0:width]), "sV")

        @P.sec("sync")
        def _(sy):
            P.wait(sy, "sV")
            P.inc(sy.dma_start(y[o * 128:(o + 1) * 128, col0:col0 + width],
                               yout[:, slot * TTW:slot * TTW + width]), stname, 16)

    emit_mm_stage(P, (wbufA, wbufB), [(pA, pB), (pC, pD)], [wop], 8, 4,
                  attnout, TF, [(i * TTW, TTW) for i in range(nTile)], epi_part, 1)

    @P.sec("sync")
    def _(sy):
        P.wait(sy, "stA")
        P.wait(sy, "stB")
    pstk3.close()
    P.pop_scope()



def build_attn_a():
    P = Prog()
    io = _io_attn_a(P.nc)
    P.start_block()
    emit_attn_a(P, io)
    P.close()
    return P.nc


def _io_moe(nc, pfx="", internal=False, act_internal=False):
    return {
        "xT": nc.dram_tensor(pfx + "xT", [D, TOK], F32, kind="ExternalInput" if not act_internal else "Internal"),
        "y": nc.dram_tensor(pfx + "y", [D, TOK], F32, kind="ExternalOutput" if not act_internal else "Internal"),
        "w1p": nc.dram_tensor(pfx + "w1p", [4, 128, 8, 128], F32, kind="ExternalInput" if not internal else "Internal"),
        "w2p": nc.dram_tensor(pfx + "w2p", [8, 128, 4, 128], F32, kind="ExternalInput" if not internal else "Internal"),
        "wgp": nc.dram_tensor(pfx + "wgp", [128, 8, 8], F32, kind="ExternalInput" if not internal else "Internal"),
        "bgd": nc.dram_tensor(pfx + "bgd", [1, 8], F32, kind="ExternalInput" if not internal else "Internal"),
        "indd": nc.dram_tensor(pfx + "indd", [8, 512], F32, kind="ExternalInput" if not internal else "Internal"),
    }


def emit_moe(P, io, pfx=""):
    nc = P.nc
    P.push_scope(pfx)
    xT = io["xT"]
    y = io["y"]
    w1p = io["w1p"]
    w2p = io["w2p"]
    wgp = io["wgp"]
    bgd = io["bgd"]
    indd = io["indd"]

    TTW = 512
    nTT = TOK // TTW

    xbuf = P.sb("xbuf", [128, 8 * TTW], F32)
    xhat = P.sb("xhat", [128, 8 * TOK], BF16)
    hbuf = P.sb("hbuf", [128, 4 * TOK], BF16)
    hwbuf = P.sb("hwbuf", [128, 4 * TOK], BF16)
    scratch = P.sb("scratch", [128, 2048], F32)
    wbufA = P.sb("wbufA", [128, 1024], BF16)
    wbufB = P.sb("wbufB", [128, 1024], BF16)
    wgs = P.sb("wgs", [128, 64], BF16)
    indic = P.sb("indic", [8, 512], BF16)
    bgb = P.sb("bgb", [1, 8], F32)
    lgsb = P.sb("lgsb", [8, TOK], F32)
    w8T = P.sb("w8T", [8, TOK], BF16)
    lgt = P.sb("lgt", [128, 8], F32)
    tpw = P.sb("tpw", [128, 8], F32)
    m12 = P.sb("m12", [128, 6], F32)
    ones_col = P.sb("ones_col", [128, 1], F32)
    ones_row = P.sb("ones_row", [1, 512], F32)
    id128 = P.sb("id128", [128, 128], F32)
    stat = P.sb("stat", [1, 4 * TTW], F32)
    xres = P.sb("xres", [128, 2 * TTW], F32)
    yout = P.sb("yout", [128, 2 * TTW], F32)

    pA = P.ps("pA", [128, 512])
    pB = P.ps("pB", [128, 512])
    pC = P.ps("pC", [128, 512])
    pD = P.ps("pD", [128, 512])
    pS1 = P.ps("pS1", [1, 512])
    pS2 = P.ps("pS2", [1, 512])
    pL = P.ps("pL", [8, 512])
    pG = P.ps("pG", [128, 144])

    P.sem("ldx", "ldc", "ldwA", "ldwB", "ldrA", "ldrB", "sP", "sV", "sA",
          "stA", "stB", "lnfree")

    id_dram = nc.inline_tensor(np.eye(128, dtype=np.float32), name=pfx + "id128d")

    @P.sec("gpsimd")
    def _(g):
        P.inc(g.dma_start(wgs[:, :], wgp[:, :, :].rearrange("p a b -> p (a b)")),
              "ldc", 16)
        P.inc(g.dma_start(indic[:, :], indd[:, :]), "ldc", 16)

    @P.sec("sync")
    def _(sy):
        P.inc(sy.dma_start(bgb[:, :], bgd[:, :]), "ldx", 16)
        P.inc(sy.dma_start(id128[:, :], id_dram[:, :]), "ldx", 16)

    @P.sec("vector")
    def _(v):
        v.memset(ones_col[:, :], 1.0)
        P.inc(v.memset(ones_row[:, :], 1.0), "lnfree")

    P.barrier()

    for tt in range(nTT):
        emit_layernorm(P, xsrc_dram=xT, tcol0=tt * TTW, tcols=TTW,
                       xbuf=xbuf, xhat=xhat, xh_col0=tt * TTW, scratch=scratch,
                       pS1=pS1, pS2=pS2, pBC1=pA, pBC2=pB,
                       ones_col=ones_col, ones_row=ones_row, stat=stat)
    P.barrier()

    sp0 = P.cnt["sP"]
    sv0 = P.cnt["sV"]

    @P.sec("tensor")
    def _(t):
        P.wait(t, "ldc")
        P.wait(t, "ldx")
        for tt in range(nTT):
            if tt >= 1:
                t.wait_ge(P.sems["sV"], sv0 + tt)
            for k in range(8):
                t.matmul(pL[0:8, 0:TTW], wgs[:, k * 8:(k + 1) * 8],
                         xhat[:, k * TOK + tt * TTW:k * TOK + (tt + 1) * TTW],
                         start=(k == 0), stop=False)
            ins = t.matmul(pL[0:8, 0:TTW], bgb[0:1, 0:8],
                           ones_row[0:1, 0:TTW], start=False, stop=True)
            P.inc(ins, "sP")

    for tt in range(nTT):
        @P.sec("vector")
        def _(v):
            v.wait_ge(P.sems["sP"], sp0 + tt + 1)
            P.inc(v.tensor_copy(lgsb[0:8, tt * TTW:(tt + 1) * TTW],
                                pL[0:8, 0:TTW]), "sV")
    P.barrier()

    sp0 = P.cnt["sP"]
    sv0 = P.cnt["sV"]
    NB = TOK // 128
    for blk in range(NB):
        @P.sec("tensor")
        def _(t):
            if blk >= 1:
                t.wait_ge(P.sems["sV"], sv0 + blk * 12)
            ins = t.transpose(pG[:, 0:8], lgsb[0:8, blk * 128:(blk + 1) * 128],
                              id128[0:8, 0:8])
            P.inc(ins, "sP")

        @P.sec("vector")
        def _(v):
            v.wait_ge(P.sems["sP"], sp0 + blk * 2 + 1)
            P.inc(v.tensor_copy(lgt[:, 0:8], pG[:, 0:8]), "sV")
            P.wait(v, "sV")
            P.inc(v.reduce_max(m12[:, 0:1], lgt[:, 0:8], axis=AX.X), "sV")
            P.wait(v, "sV")
            P.inc(v.tensor_scalar(tpw[:, 0:8], lgt[:, 0:8], m12[:, 0:1], 0.0,
                                  ALU.subtract, ALU.is_equal), "sV")
            P.wait(v, "sV")
            P.inc(v.scalar_tensor_tensor(scratch[:, 0:8], tpw[:, 0:8], -1e30,
                                         lgt[:, 0:8], ALU.mult, ALU.add), "sV")
            P.wait(v, "sV")
            P.inc(v.reduce_max(m12[:, 1:2], scratch[:, 0:8], axis=AX.X), "sV")
            P.wait(v, "sV")
            P.inc(v.tensor_tensor(m12[:, 2:3], m12[:, 0:1], m12[:, 1:2],
                                  ALU.subtract), "sV")

        @P.sec("scalar")
        def _(sc):
            P.wait(sc, "sV")
            P.inc(sc.activation(m12[:, 3:4], m12[:, 2:3], AF.Sigmoid), "sA")

        @P.sec("vector")
        def _(v):
            P.wait(v, "sA")
            P.inc(v.tensor_scalar(m12[:, 4:5], m12[:, 3:4], -1.0, 1.0,
                                  ALU.mult, ALU.add), "sV")
            P.wait(v, "sV")
            P.inc(v.tensor_scalar(scratch[:, 8:16], scratch[:, 0:8], m12[:, 1:2],
                                  0.0, ALU.subtract, ALU.is_equal), "sV")
            P.wait(v, "sV")
            P.inc(v.tensor_scalar(tpw[:, 0:8], tpw[:, 0:8], m12[:, 3:4], None,
                                  ALU.mult), "sV")
            P.inc(v.tensor_scalar(scratch[:, 8:16], scratch[:, 8:16], m12[:, 4:5],
                                  None, ALU.mult), "sV")
            P.wait(v, "sV")
            P.inc(v.tensor_tensor(tpw[:, 0:8], tpw[:, 0:8], scratch[:, 8:16],
                                  ALU.add), "sV")

        @P.sec("tensor")
        def _(t):
            P.wait(t, "sV")
            ins = t.transpose(pG[0:8, 16:144], tpw[:, 0:8], id128[:, :])
            P.inc(ins, "sP")

        @P.sec("vector")
        def _(v):
            v.wait_ge(P.sems["sP"], sp0 + blk * 2 + 2)
            P.inc(v.tensor_copy(w8T[0:8, blk * 128:(blk + 1) * 128],
                                pG[0:8, 16:144]), "sV")
        assert P.cnt["sV"] == sv0 + (blk + 1) * 12
    P.barrier()

    tiles = [(tt * TTW, TTW) for tt in range(nTT)]

    def epi_h(o, ti, tw, pair, spmark, i, sv0_):
        col0, width = tw

        @P.sec("scalar")
        def _(sc):
            sc.wait_ge(P.sems["sP"], spmark)
            P.inc(sc.activation(hbuf[:, o * TOK + col0:o * TOK + col0 + width],
                                pair[0][:, 0:width], AF.Gelu), "sA")

        @P.sec("vector")
        def _(v):
            P.wait(v, "sA")
            P.inc(v.nop(), "sV")

    emit_mm_stage(P, (wbufA, wbufB), [(pA, pB), (pC, pD)], [w1p], 4, 8,
                  xhat, TOK, tiles, epi_h, 1)
    P.barrier()

    sp0 = P.cnt["sP"]
    sv0 = P.cnt["sV"]
    banks4 = [pA, pB, pC, pD]

    @P.sec("tensor")
    def _(t):
        for o in range(4):
            for tt in range(nTT):
                i = o * nTT + tt
                if i >= 4:
                    t.wait_ge(P.sems["sV"], sv0 + i - 3)
                ins = t.matmul(banks4[i % 4][:, 0:TTW],
                               indic[0:8, o * 128:(o + 1) * 128],
                               w8T[0:8, tt * TTW:(tt + 1) * TTW],
                               start=True, stop=True)
                P.inc(ins, "sP")

    for o in range(4):
        for tt in range(nTT):
            i = o * nTT + tt

            @P.sec("vector")
            def _(v):
                v.wait_ge(P.sems["sP"], sp0 + i + 1)
                ins = v.tensor_tensor(
                    hwbuf[:, o * TOK + tt * TTW:o * TOK + (tt + 1) * TTW],
                    hbuf[:, o * TOK + tt * TTW:o * TOK + (tt + 1) * TTW],
                    banks4[i % 4][:, 0:TTW], ALU.mult)
                P.inc(ins, "sV")
    P.barrier()

    def epi_out(o, ti, tw, pair, spmark, i, sv0_):
        col0, width = tw
        slot = i % 2
        stname = "stA" if slot == 0 else "stB"
        ldrname = "ldrA" if slot == 0 else "ldrB"
        st_slot_prev = P.cnt[stname]

        @P.sec("sync")
        def _(sy):
            if st_slot_prev > 0:
                sy.wait_ge(P.sems[stname], st_slot_prev)
            P.inc(sy.dma_start(xres[:, slot * TTW:slot * TTW + width],
                               xT[o * 128:(o + 1) * 128, col0:col0 + width]),
                  ldrname, 16)

        @P.sec("vector")
        def _(v):
            v.wait_ge(P.sems["sP"], spmark)
            P.wait(v, ldrname)
            if st_slot_prev > 0:
                v.wait_ge(P.sems[stname], st_slot_prev)
            ins = v.scalar_tensor_tensor(
                yout[:, slot * TTW:slot * TTW + width], pair[0][:, 0:width],
                2.0, xres[:, slot * TTW:slot * TTW + width], ALU.mult, ALU.add)
            P.inc(ins, "sV")

        @P.sec("sync")
        def _(sy):
            P.wait(sy, "sV")
            P.inc(sy.dma_start(y[o * 128:(o + 1) * 128, col0:col0 + width],
                               yout[:, slot * TTW:slot * TTW + width]), stname, 16)

    emit_mm_stage(P, (wbufA, wbufB), [(pA, pB), (pC, pD)], [w2p], 8, 4,
                  hwbuf, TOK, tiles, epi_out, 1)

    @P.sec("sync")
    def _(sy):
        P.wait(sy, "stA")
        P.wait(sy, "stB")
    P.pop_scope()




def build_moe():
    P = Prog()
    io = _io_moe(P.nc)
    P.start_block()
    emit_moe(P, io)
    P.close()
    return P.nc


def _io_mamba(nc, pfx="", internal=False, act_internal=False):
    return {
        "xTf": nc.dram_tensor(pfx + "xTf", [D, S], F32, kind="ExternalInput" if not act_internal else "Internal"),
        "y": nc.dram_tensor(pfx + "y", [D, S], F32, kind="ExternalOutput" if not act_internal else "Internal"),
        "winxp": nc.dram_tensor(pfx + "winxp", [16, 128, 8, 128], F32, kind="ExternalInput" if not internal else "Internal"),
        "wznp": nc.dram_tensor(pfx + "wznp", [8, 128, 8, 128], F32, kind="ExternalInput" if not internal else "Internal"),
        "wxp": nc.dram_tensor(pfx + "wxp", [128, 16, 96], F32, kind="ExternalInput" if not internal else "Internal"),
        "wdtp": nc.dram_tensor(pfx + "wdtp", [64, 8, 128], F32, kind="ExternalInput" if not internal else "Internal"),
        "bdtd": nc.dram_tensor(pfx + "bdtd", [128, 8], F32, kind="ExternalInput" if not internal else "Internal"),
        "convwd": nc.dram_tensor(pfx + "convwd", [128, 16, 4], F32, kind="ExternalInput" if not internal else "Internal"),
        "convbd": nc.dram_tensor(pfx + "convbd", [128, 16], F32, kind="ExternalInput" if not internal else "Internal"),
        "acold": nc.dram_tensor(pfx + "acold", [128, 128], F32, kind="ExternalInput" if not internal else "Internal"),
        "dcold": nc.dram_tensor(pfx + "dcold", [128, 8], F32, kind="ExternalInput" if not internal else "Internal"),
        "woutp": nc.dram_tensor(pfx + "woutp", [8, 128, 8, 128], F32, kind="ExternalInput" if not internal else "Internal"),
    }


def emit_mamba(P, io, pfx=""):
    nc = P.nc
    P.push_scope(pfx)
    TF = S
    xTf = io["xTf"]
    y = io["y"]
    winxp = io["winxp"]
    wznp = io["wznp"]
    wxp = io["wxp"]
    wdtp = io["wdtp"]
    bdtd = io["bdtd"]
    convwd = io["convwd"]
    convbd = io["convbd"]
    acold = io["acold"]
    dcold = io["dcold"]
    woutp = io["woutp"]
    xbc_d = nc.dram_tensor(pfx + "xbc_d", [32, TF], F32, kind="Internal")

    TTW = 512
    nTT = TF // TTW
    ds_nchunks = 8

    LNW = 256
    xbuf = P.sb("xbuf", [128, 8 * LNW], F32)
    xhat = P.sb("xhat", [128, 8 * TTW], BF16)
    ubuf = P.sb("ubuf", [128, 8 * TF], BF16)
    utmp = P.sb("utmp", [128, 512], BF16)
    exttile = P.sb("exttile", [128, 2 * 520], F32)
    convtail = P.sb("convtail", [128, 16 * 3], F32)
    convacc = P.sb("convacc", [128, 2 * 512], F32)
    xdbl = P.sb("xdbl", [64, TF], BF16)
    bcstage = P.sb("bcstage", [128, 512], F32)
    wbufA = P.sb("wbufA", [128, 1024], BF16)
    wbufB = P.sb("wbufB", [128, 1024], BF16)
    wxs = P.sb("wxs", [128, 16 * 96], BF16)
    wdts = P.sb("wdts", [64, 8 * 128], BF16)
    smallb = P.sb("smallb", [128, 8 + 16 * 5 + 128 + 8 + 1], F32)
    bb16 = P.sb("bb16", [128, 16 * 512], BF16)
    cb16 = P.sb("cb16", [128, 16 * 512], BF16)
    deltat = P.sb("deltat", [128, 1024], F32)
    dut = P.sb("dut", [128, 1024], BF16)
    dAh = P.sb("dAh", [128, 8 * 512], F32)
    dbuh = P.sb("dbuh", [128, 8 * 512], BF16)
    hh = P.sb("hh", [128, 8 * 512], BF16)
    ysum = P.sb("ysum", [128, 512], F32)
    state = P.sb("state", [128, 128], F32)
    ytile = P.sb("ytile", [128, 8 * 512], BF16)
    scratch = P.sb("scratch", [128, 2048], F32)
    stat = P.sb("stat", [1, 4 * LNW], F32)
    ones_col = P.sb("ones_col", [128, 1], F32)
    ones_row = P.sb("ones_row", [1, 128], F32)
    yout = P.sb("yout", [128, 2 * TTW], F32)
    sps = scratch
    yzt = ytile
    yth = dbuh

    C_BDT = 0
    C_CONVB = 8
    C_CONVW = 24
    C_ACOL = 88
    C_DCOL = 216
    C_ONE = 224

    pA = P.ps("pA", [128, 512])
    pB = P.ps("pB", [128, 512])
    pC = P.ps("pC", [128, 512])
    pD = P.ps("pD", [128, 512])
    pS1 = P.ps("pS1", [1, 512])
    pS2 = P.ps("pS2", [1, 512])
    pX = P.ps("pX", [96, 512])

    P.sem("ldx", "ldc", "ldwA", "ldwB", "sP", "sV", "sA", "sG",
          "stA", "stB", "stC", "ldb", "lnfree")

    @P.sec("gpsimd")
    def _(g):
        P.inc(g.dma_start(wxs[:, :], wxp[:, :, :].rearrange("p a b -> p (a b)")),
              "ldc", 16)
        P.inc(g.dma_start(wdts[:, :], wdtp[:, :, :].rearrange("p a b -> p (a b)")),
              "ldc", 16)


    @P.sec("sync")
    def _(sy):
        P.inc(sy.dma_start(smallb[:, C_BDT:C_BDT + 8], bdtd[:, :]), "ldx", 16)
        P.inc(sy.dma_start(smallb[:, C_CONVB:C_CONVB + 16], convbd[:, :]), "ldx", 16)
        P.inc(sy.dma_start(smallb[:, C_CONVW:C_CONVW + 64],
                           convwd[:, :, :].rearrange("p a b -> p (a b)")), "ldx", 16)
        P.inc(sy.dma_start(smallb[:, C_ACOL:C_ACOL + 128], acold[:, :]), "ldx", 16)
        P.inc(sy.dma_start(smallb[:, C_DCOL:C_DCOL + 8], dcold[:, :]), "ldx", 16)

    @P.sec("vector")
    def _(v):
        v.memset(ones_col[:, :], 1.0)
        v.memset(ones_row[:, :], 1.0)
        v.memset(smallb[:, C_ONE:C_ONE + 1], 1.0)
        v.memset(state[:, :], 0.0)
        v.memset(convtail[:, :], 0.0)
        P.inc(v.drain(), "lnfree")

    @P.sec("scalar")
    def _(sc):
        P.wait(sc, "ldx")
        P.inc(sc.activation(smallb[:, C_ACOL:C_ACOL + 128],
                            smallb[:, C_ACOL:C_ACOL + 128], AF.Exp), "sA")

    @P.sec("vector")
    def _(v):
        P.wait(v, "sA")
        P.inc(v.tensor_scalar_mul(smallb[:, C_ACOL:C_ACOL + 128],
                                  smallb[:, C_ACOL:C_ACOL + 128], -1.0), "sV")

    P.barrier()

    def sbc(c0, o):
        return smallb[:, c0 + o:c0 + o + 1]

    ds_is_low = True

    _sc1 = nc.named_scope(pfx + "ph1"); _sc1.__enter__()
    for tt in range(nTT):
        for lh in range(2):
            emit_layernorm(P, xsrc_dram=xTf, tcol0=tt * TTW + lh * LNW, tcols=LNW,
                           xbuf=xbuf, xhat=xhat, xh_col0=lh * LNW, scratch=scratch,
                           pS1=pS1, pS2=pS2, pBC1=pA, pBC2=pB,
                           ones_col=ones_col, ones_row=ones_row, stat=stat)
        P.barrier()

        sp0 = P.cnt["sP"]
        sa0 = P.cnt["sA"]
        ldw0 = [P.cnt["ldwA"], P.cnt["ldwB"]]

        @P.sec("gpsimd")
        def _(g):
            for o in range(16):
                if o >= 2:
                    g.wait_ge(P.sems["sP"],
                              sp0 + (1 if o == 2 else 2 * (o - 2)))
                else:
                    g.wait_ge(P.sems["sP"], sp0)
                wb = wbufA if o % 2 == 0 else wbufB
                emit_slab_load(P, g, wb, 0, winxp, o, 8,
                               "ldwA" if o % 2 == 0 else "ldwB")

        def emit_copy_conv(c):
            in_slice = c < 8
            slot = c % 2
            exts = exttile[:, slot * 520:slot * 520 + 520]
            accs = convacc[:, slot * 512:slot * 512 + 512]

            @P.sec("vector")
            def _(v):
                v.wait_ge(P.sems["sP"], sp0 + (1 if c == 0 else 2 * c))
                if c >= 2:
                    v.wait_ge(P.sems["sV"], sv_conv[c - 2])
                pair = pA if c % 2 == 0 else pB
                P.inc(v.tensor_copy(exts[:, 3:515], pair[:, 0:512]), "sA")

            @P.sec("vector")
            def _(v):
                P.wait(v, "sA")
                v.tensor_copy(exts[:, 0:3], convtail[:, c * 3:(c + 1) * 3])
                P.inc(v.drain(), "sV")
                P.wait(v, "sV")
                P.inc(v.tensor_copy(convtail[:, c * 3:(c + 1) * 3],
                                    exts[:, 512:515]), "sV")
                P.wait(v, "sV")
                if c >= 2:
                    v.wait_ge(P.sems["sG"], sg00 + c - 1)
                v.tensor_scalar(accs[:, 0:512], exts[:, 0:512],
                                sbc(C_CONVW, 4 * c + 0), None, ALU.mult)
                P.inc(v.drain(), "sV")
                P.wait(v, "sV")
                for j in range(1, 4):
                    ins = v.scalar_tensor_tensor(
                        accs[:, 0:512], exts[:, j:j + 512],
                        sbc(C_CONVW, 4 * c + j), accs[:, 0:512],
                        ALU.mult, ALU.add)
                    P.inc(ins, "sV")
                    P.wait(v, "sV")
            sv_conv[c] = P.cnt["sV"]

        def emit_silu_xdbl(c):
            in_slice = c < 8
            slot = c % 2
            accs = convacc[:, slot * 512:slot * 512 + 512]

            @P.sec("scalar")
            def _(sc):
                sc.wait_ge(P.sems["sV"], sv_conv[c])
                if not in_slice and c > 8:
                    sc.wait_ge(P.sems["sP"], sp0 + 2 * c + 1)
                dst = (ubuf[:, c * TF + tt * 512:c * TF + (tt + 1) * 512]
                       if in_slice else utmp[:, 0:512])
                P.inc(sc.activation(dst, accs[:, 0:512], AF.Silu,
                                    bias=sbc(C_CONVB, c)), "sG")

            @P.sec("tensor")
            def _(t):
                P.wait(t, "sG")
                P.wait(t, "ldc")
                src2 = (ubuf[:, c * TF + tt * 512:c * TF + (tt + 1) * 512]
                        if in_slice else utmp[:, 0:512])
                ins = t.matmul(pX[0:96, 0:512], wxs[:, c * 96:(c + 1) * 96],
                               src2, start=(c == 0), stop=(c == 15),
                               skip_group_check=True)
                P.inc(ins, "sP")

        def emit_xs(c):
            @P.sec("tensor")
            def _(t):
                nm = "ldwA" if c % 2 == 0 else "ldwB"
                t.wait_ge(P.sems[nm], ldw0[c % 2] + (c // 2 + 1) * 16)
                wb = wbufA if c % 2 == 0 else wbufB
                if c >= 2:
                    t.wait_ge(P.sems["sA"], sa0 + c - 1)
                pair = pA if c % 2 == 0 else pB
                for k in range(8):
                    ins = t.matmul(pair[:, 0:512], wb[:, k * 128:(k + 1) * 128],
                                   xhat[:, k * TTW:(k + 1) * TTW],
                                   start=(k == 0), stop=(k == 7))
                P.inc(ins, "sP")

        sv_conv = {}
        sg00 = P.cnt["sG"]
        for c in range(16):
            emit_xs(c)
            emit_copy_conv(c)
            if c >= 1:
                emit_silu_xdbl(c - 1)
        emit_silu_xdbl(15)

        @P.sec("vector")
        def _(v):
            P.wait(v, "sP")
            P.inc(v.tensor_copy(xdbl[0:64, tt * 512:(tt + 1) * 512],
                                pX[0:64, 0:512]), "sV")
            P.wait(v, "sV")
            if P.cnt["stC"] > 0:
                v.wait_ge(P.sems["stC"], P.cnt["stC"])
            P.inc(v.tensor_copy(bcstage[64:96, 0:512], pX[64:96, 0:512]), "sV")

        @P.sec("sync")
        def _(sy):
            P.wait(sy, "sV")
            P.inc(sy.dma_start(xbc_d[0:32, tt * 512:(tt + 1) * 512],
                               bcstage[64:96, 0:512]), "stC", 16)
        P.barrier()

    _sc1.__exit__(None, None, None)
    _sc3 = nc.named_scope(pfx + "ph3"); _sc3.__enter__()
    for tc in range(nTT):

        @P.sec("gpsimd")
        def _(g):
            P.wait(g, "stC")
            bsrc = bass.AP(xbc_d.ap().tensor, tc * 512,
                           [[0, 128], [TF, 16], [1, 512]])
            csrc = bass.AP(xbc_d.ap().tensor, 16 * TF + tc * 512,
                           [[0, 128], [TF, 16], [1, 512]])
            P.inc(g.dma_start(bb16[:, :].rearrange("p (a b) -> p a b", a=16),
                              bsrc), "ldb", 16)
            P.inc(g.dma_start(cb16[:, :].rearrange("p (a b) -> p a b", a=16),
                              csrc), "ldb", 16)

        @P.sec("vector")
        def _(v):
            P.wait(v, "ldb")
            P.inc(v.nop(), "sV")
        P.barrier()

        MM3 = {}

        def emit_delta(db):
            slot = (db % 2) * 512
            pair = pA if db % 2 == 0 else pB
            spm = P.cnt["sP"] + 1

            @P.sec("tensor")
            def _(t):
                mk = MM3.get(("ts", db - 2))
                if mk is not None:
                    t.wait_ge(P.sems["sV"], mk)
                else:
                    P.wait(t, "sV")
                P.inc(t.matmul(pair[:, 0:512], wdts[:, db * 128:(db + 1) * 128],
                               xdbl[0:64, tc * 512:(tc + 1) * 512],
                               start=True, stop=True), "sP")

            @P.sec("vector")
            def _(v):
                v.wait_ge(P.sems["sP"], spm)
                mk = MM3.get(("ln", db - 1))
                if mk is not None:
                    v.wait_ge(P.sems["sA"], mk)
                MM3["ts", db] = P.inc(
                    v.tensor_scalar(sps[:, 0:512], pair[:, 0:512],
                                    sbc(C_BDT, db), None, ALU.add), "sV")

            @P.sec("vector")
            def _(v):
                v.wait_ge(P.sems["sV"], MM3["ts", db])
                MM3["abs", db] = P.inc(v.scalar_tensor_tensor(
                    sps[:, 1024:1536], sps[:, 0:512], -1.0,
                    sps[:, 0:512], ALU.mult, ALU.max), "sV")

            @P.sec("scalar")
            def _(sc):
                sc.wait_ge(P.sems["sV"], MM3["abs", db])
                mk = MM3.get(("delta", db - 1))
                if mk is not None:
                    sc.wait_ge(P.sems["sV"], mk)
                m = P.inc(sc.activation(sps[:, 512:1024], sps[:, 1024:1536],
                                        AF.Exp, scale=-1.0), "sA")
                sc.wait_ge(P.sems["sA"], m)
                MM3["ln", db] = P.inc(
                    sc.activation(sps[:, 512:1024], sps[:, 512:1024], AF.Ln,
                                  bias=smallb[:, C_ONE:C_ONE + 1]), "sA")

            @P.sec("vector")
            def _(v):
                v.wait_ge(P.sems["sA"], MM3["ln", db])
                mk = MM3.get(("exps1", db - 2))
                if mk is not None:
                    v.wait_ge(P.sems["sA"], mk)
                m = P.inc(v.tensor_scalar_max(deltat[:, slot:slot + 512],
                                              sps[:, 0:512], 0.0), "sV")
                v.wait_ge(P.sems["sV"], m)
                MM3["delta", db] = P.inc(
                    v.tensor_tensor(deltat[:, slot:slot + 512],
                                    deltat[:, slot:slot + 512],
                                    sps[:, 512:1024], ALU.add), "sV")
                v.wait_ge(P.sems["sV"], MM3["delta", db])
                MM3["dut", db] = P.inc(v.tensor_tensor(
                    dut[:, slot:slot + 512], deltat[:, slot:slot + 512],
                    ubuf[:, db * TF + tc * 512:db * TF + (tc + 1) * 512],
                    ALU.mult), "sV")

        def emit_scan(db):
            slot = (db % 2) * 512
            for g in range(2):
                sa_g0 = P.cnt["sA"]

                @P.sec("scalar")
                def _(sc):
                    sc.wait_ge(P.sems["sV"], MM3["delta", db])
                    if g == 0:
                        mk = MM3.get(("scans", db - 1))
                        if mk is not None:
                            sc.wait_ge(P.sems["sV"], mk)
                    if g == 1:
                        P.wait(sc, "sG")
                    for n in range(8):
                        P.inc(sc.activation(dAh[:, n * 512:(n + 1) * 512],
                                            deltat[:, slot:slot + 512], AF.Exp,
                                            scale=sbc(C_ACOL,
                                                      db * 16 + g * 8 + n)),
                              "sA")
                if g == 1:
                    MM3["exps1", db] = P.cnt["sA"]

                @P.sec("vector")
                def _(v):
                    v.wait_ge(P.sems["sV"], MM3["dut", db])
                    dub = bass.AP(dut.ap().tensor, dut.ap().offset + slot,
                                  [[1024, 128], [0, 8], [1, 512]])
                    m = P.inc(v.tensor_tensor(dbuh[:, 0:4096], dub,
                                              bb16[:, g * 4096:(g + 1) * 4096],
                                              ALU.mult), "sV")
                    v.wait_ge(P.sems["sV"], m)
                    for n in range(8):
                        nn = g * 8 + n
                        v.wait_ge(P.sems["sA"], sa_g0 + n + 1)
                        ins = v.tensor_tensor_scan(
                            hh[:, n * 512:(n + 1) * 512],
                            dAh[:, n * 512:(n + 1) * 512],
                            dbuh[:, n * 512:(n + 1) * 512],
                            state[:, db * 16 + nn:db * 16 + nn + 1],
                            ALU.mult, ALU.add)
                        P.inc(ins, "sV")
                    if g == 1:
                        MM3["scans", db] = P.cnt["sV"]
                    P.wait(v, "sV")
                    lastcols = bass.AP(hh.ap().tensor, hh.ap().offset + 511,
                                       [[512 * 8, 128], [512, 8]])
                    P.inc(v.tensor_copy(
                        state[:, db * 16 + g * 8:db * 16 + g * 8 + 8],
                        lastcols), "sV")
                    P.inc(v.tensor_tensor(yth[:, 0:4096], hh[:, 0:4096],
                                          cb16[:, g * 4096:(g + 1) * 4096],
                                          ALU.mult), "sV")
                    P.wait(v, "sV")
                    P.inc(v.tensor_tensor(yth[:, 0:2048], yth[:, 0:2048],
                                          yth[:, 2048:4096], ALU.add), "sV")
                    P.wait(v, "sV")
                    P.inc(v.tensor_tensor(yth[:, 0:1024], yth[:, 0:1024],
                                          yth[:, 1024:2048], ALU.add), "sV")
                    P.wait(v, "sV")
                    if g == 0:
                        ins = v.tensor_tensor(ysum[:, 0:512], yth[:, 0:512],
                                              yth[:, 512:1024], ALU.add)
                    else:
                        v.tensor_tensor(yth[:, 0:512], yth[:, 0:512],
                                        yth[:, 512:1024], ALU.add)
                        P.inc(v.drain(), "sV")
                        P.wait(v, "sV")
                        ins = v.tensor_tensor(ysum[:, 0:512], ysum[:, 0:512],
                                              yth[:, 0:512], ALU.add)
                    P.inc(ins, "sV")
                    if g == 0:
                        P.inc(v.drain(), "sG")

            @P.sec("vector")
            def _(v):
                P.wait(v, "sV")
                ins = v.scalar_tensor_tensor(
                    ytile[:, db * 512:(db + 1) * 512],
                    ubuf[:, db * TF + tc * 512:db * TF + (tc + 1) * 512],
                    sbc(C_DCOL, db), ysum[:, 0:512], ALU.mult, ALU.add)
                P.inc(ins, "sV")

        emit_delta(0)
        for db in range(8):
            emit_scan(db)
            if db < 7:
                emit_delta(db + 1)
        P.barrier()

        _sc3.__exit__(None, None, None)
        _scz = nc.named_scope(pfx + "phz"); _scz.__enter__()
        for lh in range(2):
            emit_layernorm(P, xsrc_dram=xTf, tcol0=tc * TTW + lh * LNW, tcols=LNW,
                           xbuf=xbuf, xhat=xhat, xh_col0=lh * LNW, scratch=scratch,
                           pS1=pS1, pS2=pS2, pBC1=pA, pBC2=pB,
                           ones_col=ones_col, ones_row=ones_row, stat=stat)
        P.barrier()

        sp0 = P.cnt["sP"]
        sv0 = P.cnt["sV"]
        ldw0 = [P.cnt["ldwA"], P.cnt["ldwB"]]

        @P.sec("gpsimd")
        def _(g):
            for o in range(8):
                if o >= 2:
                    g.wait_ge(P.sems["sP"], sp0 + o - 1)
                else:
                    g.wait_ge(P.sems["sP"], sp0)
                wb = wbufA if o % 2 == 0 else wbufB
                emit_slab_load(P, g, wb, 0, wznp, o, 8,
                               "ldwA" if o % 2 == 0 else "ldwB")

        @P.sec("tensor")
        def _(t):
            for o in range(8):
                nm = "ldwA" if o % 2 == 0 else "ldwB"
                t.wait_ge(P.sems[nm], ldw0[o % 2] + (o // 2 + 1) * 16)
                wb = wbufA if o % 2 == 0 else wbufB
                pair = pA if o % 2 == 0 else pB
                if o >= 2:
                    t.wait_ge(P.sems["sV"], sv0 + o - 1)
                for k in range(8):
                    ins = t.matmul(pair[:, 0:512], wb[:, k * 128:(k + 1) * 128],
                                   xhat[:, k * TTW:(k + 1) * TTW],
                                   start=(k == 0), stop=(k == 7))
                P.inc(ins, "sP")

        for o in range(8):
            @P.sec("scalar")
            def _(sc):
                sc.wait_ge(P.sems["sP"], sp0 + o + 1)
                if o >= 1:
                    P.wait(sc, "sV")
                pair = pA if o % 2 == 0 else pB
                P.inc(sc.activation(scratch[:, 0:512],
                                    pair[:, 0:512], AF.Silu), "sA")

            @P.sec("vector")
            def _(v):
                P.wait(v, "sA")
                ins = v.tensor_tensor(yzt[:, o * 512:(o + 1) * 512],
                                      ytile[:, o * 512:(o + 1) * 512],
                                      scratch[:, 0:512], ALU.mult)
                P.inc(ins, "sV")
        P.barrier()

        sp0 = P.cnt["sP"]
        svo = P.cnt["sV"]
        ldw0 = [P.cnt["ldwA"], P.cnt["ldwB"]]

        @P.sec("gpsimd")
        def _(g):
            for o in range(8):
                if o >= 2:
                    g.wait_ge(P.sems["sP"], sp0 + o - 1)
                else:
                    g.wait_ge(P.sems["sP"], sp0)
                wb = wbufA if o % 2 == 0 else wbufB
                emit_slab_load(P, g, wb, 0, woutp, o, 8,
                               "ldwA" if o % 2 == 0 else "ldwB")

        @P.sec("tensor")
        def _(t):
            for o in range(8):
                nm = "ldwA" if o % 2 == 0 else "ldwB"
                t.wait_ge(P.sems[nm], ldw0[o % 2] + (o // 2 + 1) * 16)
                wb = wbufA if o % 2 == 0 else wbufB
                if o >= 2:
                    t.wait_ge(P.sems["sV"], svo + o - 1)
                for k in range(8):
                    ins = t.matmul((pA if o % 2 == 0 else pB)[:, 0:512],
                                   wb[:, k * 128:(k + 1) * 128],
                                   yzt[:, k * 512:(k + 1) * 512],
                                   start=(k == 0), stop=(k == 7))
                P.inc(ins, "sP")

        for o in range(8):
            slot = o % 2
            stname = "stA" if slot == 0 else "stB"
            st_prev = P.cnt[stname]

            @P.sec("vector")
            def _(v):
                v.wait_ge(P.sems["sP"], sp0 + o + 1)
                if st_prev > 0:
                    v.wait_ge(P.sems[stname], st_prev)
                ins = v.tensor_copy(yout[:, slot * 512:(slot + 1) * 512],
                                    (pA if o % 2 == 0 else pB)[:, 0:512])
                P.inc(ins, "sV")

            @P.sec("sync")
            def _(sy):
                P.wait(sy, "sV")
                P.inc(sy.dma_start(y[o * 128:(o + 1) * 128,
                                     tc * 512:(tc + 1) * 512],
                                   yout[:, slot * 512:(slot + 1) * 512]), stname, 16)
        P.barrier()
        if tc < nTT - 1:
            _scz.__exit__(None, None, None)
            _sc3 = nc.named_scope(pfx + "ph3"); _sc3.__enter__()

    _scz.__exit__(None, None, None)

    @P.sec("sync")
    def _(sy):
        P.wait(sy, "stA")
        P.wait(sy, "stB")
    P.pop_scope()




def build_mamba():
    P = Prog()
    io = _io_mamba(P.nc)
    P.start_block()
    emit_mamba(P, io)
    P.close()
    return P.nc


def _io_wkv(nc, pfx="", internal=False, act_internal=False):
    return {
        "xTf": nc.dram_tensor(pfx + "xTf", [D, S], F32, kind="ExternalInput" if not act_internal else "Internal"),
        "y": nc.dram_tensor(pfx + "y", [D, S], F32, kind="ExternalOutput" if not act_internal else "Internal"),
        "ap": nc.dram_tensor(pfx + "ap", [2, 128, 8, 128], F32, kind="ExternalInput" if not internal else "Internal"),
        "bpp": nc.dram_tensor(pfx + "bpp", [5, 8, 32, 128], F32, kind="ExternalInput" if not internal else "Internal"),
        "tdap": nc.dram_tensor(pfx + "tdap", [1, 128, 8, 128], F32, kind="ExternalInput" if not internal else "Internal"),
        "tdbp": nc.dram_tensor(pfx + "tdbp", [4, 128, 1, 128], F32, kind="ExternalInput" if not internal else "Internal"),
        "wg1p": nc.dram_tensor(pfx + "wg1p", [2, 128, 8, 128], F32, kind="ExternalInput" if not internal else "Internal"),
        "wg2p": nc.dram_tensor(pfx + "wg2p", [4, 128, 2, 128], F32, kind="ExternalInput" if not internal else "Internal"),
        "wkp": nc.dram_tensor(pfx + "wkp", [4, 128, 8, 128], F32, kind="ExternalInput" if not internal else "Internal"),
        "wvp": nc.dram_tensor(pfx + "wvp", [4, 128, 8, 128], F32, kind="ExternalInput" if not internal else "Internal"),
        "wrp": nc.dram_tensor(pfx + "wrp", [4, 128, 8, 128], F32, kind="ExternalInput" if not internal else "Internal"),
        "wop": nc.dram_tensor(pfx + "wop", [8, 128, 4, 128], F32, kind="ExternalInput" if not internal else "Internal"),
        "miud": nc.dram_tensor(pfx + "miud", [128, 8], F32, kind="ExternalInput" if not internal else "Internal"),
        "lamd": nc.dram_tensor(pfx + "lamd", [128, 40], F32, kind="ExternalInput" if not internal else "Internal"),
        "tdmd": nc.dram_tensor(pfx + "tdmd", [128, 4], F32, kind="ExternalInput" if not internal else "Internal"),
        "ud": nc.dram_tensor(pfx + "ud", [128, 4], F32, kind="ExternalInput" if not internal else "Internal"),
        "gamd": nc.dram_tensor(pfx + "gamd", [128, 4], F32, kind="ExternalInput" if not internal else "Internal"),
        "betd": nc.dram_tensor(pfx + "betd", [128, 4], F32, kind="ExternalInput" if not internal else "Internal"),
        "maskd": nc.dram_tensor(pfx + "maskd", [16, 2048], F32, kind="ExternalInput" if not internal else "Internal"),
        "ind2d": nc.dram_tensor(pfx + "ind2d", [2, 128], F32, kind="ExternalInput" if not internal else "Internal"),
    }


def emit_wkv(P, io, pfx=""):
    nc = P.nc
    P.push_scope(pfx)
    TF = S
    xTf = io["xTf"]
    y = io["y"]
    ap_ = io["ap"]
    bpp = io["bpp"]
    tdap = io["tdap"]
    tdbp = io["tdbp"]
    wg1p = io["wg1p"]
    wg2p = io["wg2p"]
    wkp = io["wkp"]
    wvp = io["wvp"]
    wrp = io["wrp"]
    wop = io["wop"]
    miud = io["miud"]
    lamd = io["lamd"]
    tdmd = io["tdmd"]
    ud = io["ud"]
    gamd = io["gamd"]
    betd = io["betd"]
    maskd = io["maskd"]
    ind2d = io["ind2d"]
    k_d = nc.dram_tensor(pfx + "k_d", [512, TF], BF16, kind="Internal")
    v_d = nc.dram_tensor(pfx + "v_d", [512, TF], BF16, kind="Internal")
    r_d = nc.dram_tensor(pfx + "r_d", [512, TF], BF16, kind="Internal")
    g_d = nc.dram_tensor(pfx + "g_d", [512, TF], BF16, kind="Internal")
    w_d = nc.dram_tensor(pfx + "w_d", [512, TF], F32, kind="Internal")
    pt_d = nc.dram_tensor(pfx + "pt_d", [2, 2, 128, 128], BF16, kind="Internal")

    TTW = 512
    nTT = TF // TTW
    LNW = 256
    XW = TTW + 1

    xbuf = P.sb("xbuf", [128, 8 * LNW], F32)
    xhat = P.sb("xhat", [128, 8 * XW], BF16)
    xcarry = P.sb("xcarry", [128, 8], BF16)
    ba = P.sb("ba", [128, 8 * TTW], BF16)
    x5 = P.sb("x5", [128, 8 * TTW], BF16)
    h5 = P.sb("h5", [128, 2 * TTW], BF16)
    tnb = P.sb("tnb", [128, TTW], BF16)
    tng = P.sb("tng", [128, 2 * TTW], BF16)
    wtile = P.sb("wtile", [128, 4 * TTW], F32)
    ktile = P.sb("ktile", [128, 4 * TTW], BF16)
    scratch = P.sb("scratch", [128, 2048], F32)
    wbufA = P.sb("wbufA", [128, 2048], BF16)
    wbufB = P.sb("wbufB", [128, 2048], BF16)
    smallb = P.sb("smallb", [128, 64], F32)
    C_MIU, C_LAM, C_TDM, C_U, C_GAM, C_BET = 0, 8, 48, 52, 56, 60
    stat = P.sb("stat", [1, 4 * LNW], F32)
    ones_col = P.sb("ones_col", [128, 1], F32)
    ones_row = P.sb("ones_row", [1, 128], F32)
    onesblk = P.sb("onesblk", [128, 2], F32)
    onesblkb = P.sb("onesblkb", [128, 2], BF16)
    ind2 = P.sb("ind2", [2, 128], F32)
    ind2b = P.sb("ind2b", [2, 128], BF16)
    maskall = P.sb("maskall", [16, 2048], F32)
    zb = P.sb("zb", [128, 128], F32)
    idbf = P.sb("idbf", [128, 128], BF16)
    k2 = P.sb("k2", [128, 384], BF16)
    r2 = P.sb("r2", [128, 384], BF16)
    v2 = P.sb("v2", [128, 384], BF16)
    g2 = P.sb("g2", [128, 384], BF16)
    w2 = P.sb("w2", [128, 384], F32)
    wcb = P.sb("wcb", [128, 129], F32)
    arq = P.sb("arq", [128, 2048], F32)
    eeb = P.sb("eeb", [128, 2048], BF16)
    rall = P.sb("rall", [128, 2048], BF16)
    ark = P.sb("ark", [128, 128], F32)
    ekb = P.sb("ekb", [128, 128], BF16)
    ktl = P.sb("ktl", [128, 128], BF16)
    kblk = P.sb("kblk", [128, 256], BF16)
    ptile = P.sb("ptile", [16, 2048], BF16)
    ptA = P.sb("ptA", [128, 256], BF16)
    ptB = P.sb("ptB", [128, 256], BF16)
    v2T = P.sb("v2T", [128, 256], BF16)
    kut = P.sb("kut", [128, 128], BF16)
    zt = P.sb("zt", [128, 256], BF16)
    efb = P.sb("efb", [128, 128], BF16)
    khat = P.sb("khat", [128, 128], BF16)
    khT = P.sb("khT", [128, 256], BF16)
    r0sl = P.sb("r0sl", [128, 256], BF16)
    sblk2 = P.sb("sblk2", [128, 256], F32)
    sbf2 = P.sb("sbf2", [128, 256], BF16)
    fcol = P.sb("fcol", [128, 2], F32)
    d2row = P.sb("d2row", [2, 128], BF16)
    ysb = P.sb("ysb", [128, 128], F32)
    ysq = P.sb("ysq", [128, 128], F32)
    gnst = P.sb("gnst", [2, 8 * 128], F32)
    gybuf = P.sb("gybuf", [128, 4 * TF], BF16)
    yout = P.sb("yout", [128, 2 * TTW], F32)

    P.sem("ldx", "ldc", "ldwA", "ldwB", "sP", "sV", "sA", "sG",
          "stA", "stB", "stC", "ldb", "ldpt", "lnfree",
          "gst", "gld0", "gld1", "ldx0", "ldx1", "ldx2")

    id_dram = nc.inline_tensor(np.eye(128, dtype=np.float32), name=pfx + "idw")

    @P.sec("gpsimd")
    def _(g):
        P.inc(g.dma_start(idbf[:, :], id_dram[:, :]), "ldb", 16)
        P.inc(g.dma_start(ind2b[:, :], ind2d[:, :]), "ldb", 16)

    @P.sec("sync")
    def _(sy):
        for c0, dt_ in ((C_MIU, miud), (C_LAM, lamd), (C_TDM, tdmd), (C_U, ud),
                        (C_GAM, gamd), (C_BET, betd)):
            P.inc(sy.dma_start(smallb[:, c0:c0 + dt_.shape[1]], dt_[:, :]),
                  "ldc", 16)
        P.inc(sy.dma_start(maskall[:, :], maskd[:, :]), "ldc", 16)
        P.inc(sy.dma_start(ind2[:, :], ind2d[:, :]), "ldc", 16)

    @P.sec("vector")
    def _(v):
        v.memset(ones_col[:, :], 1.0)
        v.memset(ones_row[:, :], 1.0)
        v.memset(onesblk[:, :], 0.0)
        v.memset(onesblkb[:, :], 0.0)
        v.memset(xcarry[:, :], 0.0)
        v.memset(zb[:, :], 0.0)
        P.inc(v.drain(), "sV")
        P.wait(v, "sV")
        v.memset(onesblk[0:64, 0:1], 1.0)
        v.memset(onesblk[64:128, 1:2], 1.0)
        v.memset(onesblkb[0:64, 0:1], 1.0)
        v.memset(onesblkb[64:128, 1:2], 1.0)
        P.inc(v.drain(), "lnfree")
    P.barrier()

    def scol(c0, o):
        return smallb[:, c0 + o:c0 + o + 1]

    xtok = lambda c: xhat[:, c * XW + 1:c * XW + 1 + TTW]

    _wsa = nc.named_scope(pfx + "phA"); _wsa.__enter__()
    with contextlib.ExitStack() as pstk:
        pA = pstk.enter_context(nc.psum_tensor(pfx + "pA", [128, 512], F32))
        pB = pstk.enter_context(nc.psum_tensor(pfx + "pB", [128, 512], F32))
        pC = pstk.enter_context(nc.psum_tensor(pfx + "pC", [128, 512], F32))
        pD = pstk.enter_context(nc.psum_tensor(pfx + "pD", [128, 512], F32))
        pS1 = pstk.enter_context(nc.psum_tensor(pfx + "pS1", [1, 512], F32))
        pS2 = pstk.enter_context(nc.psum_tensor(pfx + "pS2", [1, 512], F32))

        for tt in range(nTT):
            for lh in range(2):
                emit_layernorm(P, xsrc_dram=xTf, tcol0=tt * TTW + lh * LNW,
                               tcols=LNW, xbuf=xbuf, xhat=xhat,
                               xh_col0=1 + lh * LNW, xh_w=XW, scratch=scratch,
                               pS1=pS1, pS2=pS2, pBC1=pA, pBC2=pB,
                               ones_col=ones_col, ones_row=ones_row, stat=stat)
            carry_dst = bass.AP(xhat.ap().tensor, xhat.ap().offset,
                                [[8 * XW, 128], [XW, 8]])
            carry_src = bass.AP(xhat.ap().tensor, xhat.ap().offset + TTW,
                                [[8 * XW, 128], [XW, 8]])

            @P.sec("vector")
            def _(v):
                P.wait(v, "sV")
                P.wait(v, "ldc")
                P.inc(v.tensor_copy(carry_dst, xcarry[:, :]), "sV")
                P.wait(v, "sV")
                P.inc(v.tensor_copy(xcarry[:, :], carry_src), "sV")
                P.wait(v, "sV")
                for c in range(8):
                    ins = v.tensor_tensor(ba[:, c * TTW:(c + 1) * TTW],
                                          xhat[:, c * XW:c * XW + TTW],
                                          xtok(c), ALU.subtract)
                    P.inc(ins, "sV")
                    P.wait(v, "sV")
                for c in range(8):
                    ins = v.scalar_tensor_tensor(
                        x5[:, c * TTW:(c + 1) * TTW], ba[:, c * TTW:(c + 1) * TTW],
                        scol(C_MIU, c), xtok(c), ALU.mult, ALU.add)
                    P.inc(ins, "sV")
                    if c < 7:
                        P.wait(v, "sV")
            P.barrier()

            def mk_epi_tanh(dstbuf, dst_w):
                def epi(o, ti, tw, pair, spmark, i, sv0_):
                    col0, width = tw

                    @P.sec("scalar")
                    def _(sc):
                        sc.wait_ge(P.sems["sP"], spmark)
                        P.inc(sc.activation(
                            dstbuf[:, o * dst_w + col0:o * dst_w + col0 + width],
                            pair[0][:, 0:width], AF.Tanh), "sA")

                    @P.sec("vector")
                    def _(v):
                        P.wait(v, "sA")
                        P.inc(v.nop(), "sV")
                return epi

            banksA = [(pA, pB), (pC, pD)]
            emit_mm_stage(P, (wbufA, wbufB), banksA, [ap_], 2, 8,
                          x5, TTW, [(0, TTW)], mk_epi_tanh(h5, TTW), 1)
            P.barrier()

            for s in range(5):
                h5row = (s % 4) * 32
                h5chunk = s // 4
                sp0 = P.cnt["sP"]
                ldw0 = P.cnt["ldwA"]
                sv0 = P.cnt["sV"]

                @P.sec("gpsimd")
                def _(g):
                    g.wait_ge(P.sems["sP"], sp0)
                    src = bpp[s].rearrange("o r c -> r o c")
                    P.inc(g.dma_start(
                        wbufA[h5row:h5row + 32, 0:1024]
                        .rearrange("p (o c) -> p o c", o=8), src), "ldwA", 16)

                @P.sec("tensor")
                def _(t):
                    t.wait_ge(P.sems["ldwA"], ldw0 + 16)
                    for o in range(8):
                        if o >= 2:
                            t.wait_ge(P.sems["sV"], sv0 + 2 * (o - 1))
                        pair = pA if o % 2 == 0 else pB
                        ins = t.matmul(
                            pair[:, 0:TTW],
                            wbufA[h5row:h5row + 32, o * 128:(o + 1) * 128],
                            h5[h5row:h5row + 32, h5chunk * TTW:(h5chunk + 1) * TTW],
                            start=True, stop=True,
                            tile_position=(h5row, 0) if h5row == 96 else None)
                        P.inc(ins, "sP")

                for o in range(8):
                    @P.sec("vector")
                    def _(v):
                        v.wait_ge(P.sems["sP"], sp0 + o + 1)
                        pair = pA if o % 2 == 0 else pB
                        P.inc(v.scalar_tensor_tensor(
                            x5[:, o * TTW:(o + 1) * TTW], pair[:, 0:TTW],
                            scol(C_LAM, s * 8 + o), ba[:, o * TTW:(o + 1) * TTW],
                            ALU.add, ALU.mult), "sV")
                        P.wait(v, "sV")
                        ins = v.tensor_tensor(
                            x5[:, o * TTW:(o + 1) * TTW],
                            x5[:, o * TTW:(o + 1) * TTW], xtok(o), ALU.add)
                        P.inc(ins, "sV")
                P.barrier()

                if s == 0:
                    emit_mm_stage(P, (wbufA, wbufB), banksA, [tdap], 1, 8,
                                  x5, TTW, [(0, TTW)], mk_epi_tanh(tnb, TTW), 1)
                    P.barrier()

                    def epi_w(o, ti, tw, pair, spmark, i, sv0_):
                        @P.sec("vector")
                        def _(v):
                            v.wait_ge(P.sems["sP"], spmark)
                            P.wait(v, "stC")
                            ins = v.tensor_scalar(wtile[:, o * TTW:(o + 1) * TTW],
                                                  pair[0][:, 0:TTW],
                                                  scol(C_TDM, o), None, ALU.add)
                            P.inc(ins, "sV")

                    emit_mm_stage(P, (wbufA, wbufB), banksA, [tdbp], 4, 1,
                                  tnb, TTW, [(0, TTW)], epi_w, 1)
                    P.barrier()

                    @P.sec("sync")
                    def _(sy):
                        P.wait(sy, "sV")
                        for c in range(4):
                            P.inc(sy.dma_start(
                                w_d[c * 128:(c + 1) * 128, tt * TTW:(tt + 1) * TTW],
                                wtile[:, c * TTW:(c + 1) * TTW]), "stC", 16)
                else:
                    wd_, dst_ = {1: (wkp, k_d), 2: (wvp, v_d), 3: (wrp, r_d),
                                 4: (None, g_d)}[s]
                    if s == 4:
                        emit_mm_stage(P, (wbufA, wbufB), banksA, [wg1p], 2, 8,
                                      x5, TTW, [(0, TTW)], mk_epi_tanh(tng, TTW), 1)
                        P.barrier()

                    def epi_kvrg(o, ti, tw, pair, spmark, i, sv0_):
                        if s == 1:
                            @P.sec("vector")
                            def _(v):
                                v.wait_ge(P.sems["sP"], spmark)
                                P.wait(v, "stC")
                                P.inc(v.tensor_scalar_min(
                                    scratch[:, 0:TTW],
                                    wtile[:, o * TTW:(o + 1) * TTW], 0.0), "sV")

                            @P.sec("scalar")
                            def _(sc):
                                P.wait(sc, "sV")
                                P.inc(sc.activation(scratch[:, TTW:2 * TTW],
                                                    scratch[:, 0:TTW], AF.Exp),
                                      "sA")

                            @P.sec("vector")
                            def _(v):
                                P.wait(v, "sA")
                                P.wait(v, "stC")
                                ins = v.tensor_tensor(
                                    ktile[:, o * TTW:(o + 1) * TTW],
                                    pair[0][:, 0:TTW], scratch[:, TTW:2 * TTW],
                                    ALU.mult)
                                P.inc(ins, "sV")
                        else:
                            @P.sec("scalar")
                            def _(sc):
                                sc.wait_ge(P.sems["sP"], spmark)
                                P.wait(sc, "stC")
                                P.inc(sc.activation(ktile[:, o * TTW:(o + 1) * TTW],
                                                    pair[0][:, 0:TTW], AF.Copy),
                                      "sA")

                            @P.sec("vector")
                            def _(v):
                                P.wait(v, "sA")
                                P.inc(v.nop(), "sV")

                    if s == 4:
                        emit_mm_stage(P, (wbufA, wbufB), banksA, [wg2p], 4, 2,
                                      tng, TTW, [(0, TTW)], epi_kvrg,
                                      1)
                    else:
                        emit_mm_stage(P, (wbufA, wbufB), banksA, [wd_], 4, 8,
                                      x5, TTW, [(0, TTW)], epi_kvrg,
                                      2 if s == 1 else 1)
                    P.barrier()

                    @P.sec("sync")
                    def _(sy):
                        P.wait(sy, "sV")
                        P.wait(sy, "sA")
                        for c in range(4):
                            P.inc(sy.dma_start(
                                dst_[c * 128:(c + 1) * 128, tt * TTW:(tt + 1) * TTW],
                                ktile[:, c * TTW:(c + 1) * TTW]), "stC", 16)
            P.barrier()

    _wsa.__exit__(None, None, None)
    _wsc = nc.named_scope(pfx + "phC"); _wsc.__enter__()
    NCH = TF // 128
    NGI = 4 * NCH
    with contextlib.ExitStack() as pstk:
        pStrip = pstk.enter_context(nc.psum_tensor(pfx + "pStrip", [16, 1024], F32))
        pY = pstk.enter_context(nc.psum_tensor(pfx + "pY", [128, 128], F32))
        pDS = pstk.enter_context(nc.psum_tensor(pfx + "pDS", [128, 128], F32))
        pT = pstk.enter_context(nc.psum_tensor(pfx + "pT", [128, 128], BF16))
        pGN = pstk.enter_context(nc.psum_tensor(pfx + "pGN", [2, 512], F32))
        pM = pstk.enter_context(nc.psum_tensor(pfx + "pM", [128, 384], F32))

        M = {}
        stC_snap = P.cnt["stC"]
        ldb_snap = P.cnt["ldb"]

        @P.sec("vector")
        def _(v):
            P.wait(v, "sV")
            P.inc(v.memset(sblk2[:, :], 0.0), "sV")
            P.inc(v.memset(wcb[:, 0:1], 0.0), "sV")
            M["sbf", 0] = P.inc(v.memset(sbf2[:, :], 0.0), "sV")

        def emit_ld(gi):
            hp_, ch_ = gi // NCH, gi % NCH
            sname = "ldx%d" % (gi % 3)
            sl = (gi % 3) * 128

            @P.sec("sync")
            def _(sy):
                if gi < 3:
                    sy.wait_ge(P.sems["stC"], stC_snap)
                else:
                    sy.wait_ge(P.sems["sV"], M["final", gi - 3])
                    sy.wait_ge(P.sems["sP"], M["tp1", gi - 3])
                for dst_, src_ in ((k2, k_d), (r2, r_d), (v2, v_d), (g2, g_d),
                                   (w2, w_d)):
                    P.inc(sy.dma_start(dst_[:, sl:sl + 128],
                                       src_[hp_ * 128:(hp_ + 1) * 128,
                                            ch_ * 128:(ch_ + 1) * 128]),
                          sname, 16)
            M["ldx", gi] = P.cnt[sname]

        def emit_H(gi):
            hp_ = gi // NCH
            l3 = (gi % 3) * 128
            s2_ = (gi % 2) * 128
            WST = 129

            @P.sec("vector")
            def _(v):
                if gi >= 1:
                    v.wait_ge(P.sems["sA"], M["exp", gi - 1])
                else:
                    v.wait_ge(P.sems["sV"], M["sbf", 0])
                v.wait_ge(P.sems["ldx%d" % (gi % 3)], M["ldx", gi])
                m = P.inc(v.tensor_tensor_scan(wcb[:, 1:129],
                                               w2[:, l3:l3 + 128], zb[:, :],
                                               wcb[:, 0:1], ALU.add, ALU.add),
                          "sV")
                v.wait_ge(P.sems["sV"], m)
                aq = arq[:, :].rearrange("p (q t) -> p q t", q=16)
                in0 = bass.AP(wcb.ap().tensor, wcb.ap().offset,
                              [[WST, 128], [0, 16], [1, 128]])
                in1 = bass.AP(wcb.ap().tensor, wcb.ap().offset,
                              [[WST, 128], [8, 16], [0, 128]])
                m = P.inc(v.tensor_tensor(aq, in0, in1, ALU.subtract), "sV")
                v.wait_ge(P.sems["sV"], m)
                P.inc(v.tensor_scalar_min(arq[:, :], arq[:, :], 30.0), "sV")
                ak = ark[:, :].rearrange("p (q t) -> p q t", q=16)
                k0 = bass.AP(wcb.ap().tensor, wcb.ap().offset + 1,
                             [[WST, 128], [8, 16], [1, 8]])
                k1 = bass.AP(wcb.ap().tensor, wcb.ap().offset,
                             [[WST, 128], [8, 16], [0, 8]])
                P.inc(v.tensor_tensor(ak, k0, k1, ALU.subtract), "sV")
                M["sg", gi] = P.inc(v.drain(), "sG")

            @P.sec("scalar")
            def _(sc):
                sc.wait_ge(P.sems["sG"], M["sg", gi])
                if gi >= 2:
                    sc.wait_ge(P.sems["sV"], M["state", gi - 2])
                P.inc(sc.activation(eeb[:, :], arq[:, :], AF.Exp), "sA")
                P.inc(sc.activation(ekb[:, :], ark[:, :], AF.Exp, scale=-1.0),
                      "sA")
                P.inc(sc.activation(efb[:, :], wcb[:, 1:129], AF.Exp,
                                    scale=-1.0, bias=wcb[:, 128:129]), "sA")
                M["exp", gi] = P.inc(
                    sc.activation(fcol[:, gi % 2:gi % 2 + 1],
                                  wcb[:, 128:129], AF.Exp), "sA")

            @P.sec("vector")
            def _(v):
                v.wait_ge(P.sems["sA"], M["exp", gi])
                if gi >= 1:
                    v.wait_ge(P.sems["sP"], M["strip1", gi - 1])
                    v.wait_ge(P.sems["sP"], M["tp2", gi - 1])
                if gi >= 2:
                    v.wait_ge(P.sems["sP"], M["Y", gi - 2])
                    v.wait_ge(P.sems["sP"], M["D2", gi - 2])
                rbc = bass.AP(r2.ap().tensor, r2.ap().offset + l3,
                              [[384, 128], [0, 16], [1, 128]])
                m = P.inc(v.tensor_tensor(
                    rall[:, :].rearrange("p (a b) -> p a b", a=16),
                    eeb[:, :].rearrange("p (a b) -> p a b", a=16),
                    rbc, ALU.mult), "sV")
                v.wait_ge(P.sems["sV"], m)
                P.inc(v.tensor_copy(r0sl[:, s2_:s2_ + 128], rall[:, 0:128]),
                      "sV")
                P.inc(v.tensor_tensor(ktl[:, :], k2[:, l3:l3 + 128], ekb[:, :],
                                      ALU.mult), "sV")
                M["khat", gi] = P.inc(
                    v.tensor_tensor(khat[:, :], k2[:, l3:l3 + 128], efb[:, :],
                                    ALU.mult), "sV")
                m = P.inc(v.memset(kblk[:, :], 0.0), "sV")
                v.wait_ge(P.sems["sV"], m)
                P.inc(v.tensor_copy(
                    kblk[0:64, :].rearrange("p (a b) -> p a b", b=16)[:, :, 0:8],
                    ktl[0:64, :].rearrange("p (a b) -> p a b", b=8)), "sV")
                M["kblk", gi] = P.inc(v.tensor_copy(
                    kblk[64:128, :].rearrange("p (a b) -> p a b", b=16)[:, :, 8:16],
                    ktl[64:128, :].rearrange("p (a b) -> p a b", b=8)), "sV")
                m = P.inc(v.tensor_scalar(kut[:, :], k2[:, l3:l3 + 128],
                                          scol(C_U, hp_), None, ALU.mult), "sV")
                v.wait_ge(P.sems["sV"], m)
                M["zt", gi] = P.inc(v.tensor_tensor(zt[:, s2_:s2_ + 128],
                                                    kut[:, :],
                                                    r2[:, l3:l3 + 128],
                                                    ALU.mult), "sV")

            sp_base = P.cnt["sP"]
            M["strip0", gi] = sp_base + 8
            M["strip1", gi] = sp_base + 16

            @P.sec("vector")
            def _(v):
                v.wait_ge(P.sems["sP"], M["strip0", gi])
                if gi >= 1:
                    v.wait_ge(P.sems["gst"], M["st", gi - 1])
                M["ev0", gi] = P.inc(v.tensor_tensor(
                    ptile[0:16, 0:1024], pStrip[0:16, 0:1024],
                    maskall[0:16, 0:1024], ALU.mult), "sV")
                v.wait_ge(P.sems["sP"], M["strip1", gi])
                M["ev1", gi] = P.inc(v.tensor_tensor(
                    ptile[0:16, 1024:2048], pStrip[0:16, 0:1024],
                    maskall[0:16, 1024:2048], ALU.mult), "sV")

            @P.sec("tensor")
            def _(t):
                t.wait_ge(P.sems["sV"], M["kblk", gi])
                if gi >= 1:
                    t.wait_ge(P.sems["sV"], M["ev1", gi - 1])
                for q in range(8):
                    P.inc(t.matmul(pStrip[0:16, q * 128:(q + 1) * 128],
                                   kblk[:, q * 16:(q + 1) * 16],
                                   rall[:, q * 128:(q + 1) * 128],
                                   start=True, stop=True), "sP")
                t.wait_ge(P.sems["sV"], M["ev0", gi])
                for q in range(8, 16):
                    P.inc(t.matmul(pStrip[0:16, (q - 8) * 128:(q - 7) * 128],
                                   kblk[:, q * 16:(q + 1) * 16],
                                   rall[:, q * 128:(q + 1) * 128],
                                   start=True, stop=True), "sP")
            assert P.cnt["sP"] == sp_base + 16

            @P.sec("gpsimd")
            def _(g):
                g.wait_ge(P.sems["sV"], M["ev1", gi])
                if gi >= 2:
                    g.wait_ge(P.sems["sP"], M["Y", gi - 2])
                    g.wait_ge(P.sems["gld%d" % (gi % 2)], M["ldpt", gi - 2])
                for hh in range(2):
                    P.inc(g.dma_start(
                        pt_d[gi % 2, hh].rearrange("(q s) t -> s q t", s=8),
                        ptile[hh * 8:(hh + 1) * 8, :]
                        .rearrange("s (q t) -> s q t", q=16)), "gst", 16)
                M["st", gi] = P.cnt["gst"]
                g.wait_ge(P.sems["gst"], M["st", gi])
                gldn = "gld%d" % (gi % 2)
                P.inc(g.dma_start(ptA[:, s2_:s2_ + 128], pt_d[gi % 2, 0]),
                      gldn, 16)
                P.inc(g.dma_start(ptB[:, s2_:s2_ + 128], pt_d[gi % 2, 1]),
                      gldn, 16)
                M["ldpt", gi] = P.cnt[gldn]

            @P.sec("tensor")
            def _(t):
                if gi == 0:
                    t.wait_ge(P.sems["ldb"], ldb_snap)
                t.wait_ge(P.sems["ldx%d" % (gi % 3)], M["ldx", gi])
                if gi >= 1:
                    t.wait_ge(P.sems["sV"], M["cp2", gi - 1])
                M["tp1", gi] = P.inc(t.transpose(pT[:, :], v2[:, l3:l3 + 128],
                                                 idbf[:, :]), "sP")

            @P.sec("vector")
            def _(v):
                v.wait_ge(P.sems["sP"], M["tp1", gi])
                if gi >= 2:
                    v.wait_ge(P.sems["sP"], M["dS", gi - 2])
                M["cp1", gi] = P.inc(v.tensor_copy(v2T[:, s2_:s2_ + 128],
                                                   pT[:, :]), "sV")

            @P.sec("tensor")
            def _(t):
                t.wait_ge(P.sems["sV"], M["khat", gi])
                t.wait_ge(P.sems["sV"], M["cp1", gi])
                M["tp2", gi] = P.inc(t.transpose(pT[:, :], khat[:, :],
                                                 idbf[:, :]), "sP")

            @P.sec("vector")
            def _(v):
                v.wait_ge(P.sems["sP"], M["tp2", gi])
                M["cp2", gi] = P.inc(v.tensor_copy(khT[:, s2_:s2_ + 128],
                                                   pT[:, :]), "sV")

        def emit_T(gi):
            hp_, ch_ = gi // NCH, gi % NCH
            l3 = (gi % 3) * 128
            s2_ = (gi % 2) * 128
            nsl = ((gi + 1) % 2) * 128

            @P.sec("tensor")
            def _(t):
                t.wait_ge(P.sems["sV"], M["sbf", gi])
                t.wait_ge(P.sems["sV"], M["cp2", gi])
                t.wait_ge(P.sems["gld%d" % (gi % 2)], M["ldpt", gi])
                if gi >= 1:
                    t.wait_ge(P.sems["sV"], M["ysb", gi - 1])
                t.matmul(pY[:, :], sbf2[:, s2_:s2_ + 128],
                         r0sl[:, s2_:s2_ + 128],
                         start=True, stop=False, skip_group_check=True)
                t.matmul(pY[0:64, :], v2T[:, s2_:s2_ + 64],
                         ptA[:, s2_:s2_ + 128],
                         start=False, stop=False, skip_group_check=True)
                M["Y", gi] = P.inc(
                    t.matmul(pY[64:128, :], v2T[:, s2_ + 64:s2_ + 128],
                             ptB[:, s2_:s2_ + 128],
                             start=False, stop=True, skip_group_check=True),
                    "sP")
                t.wait_ge(P.sems["sV"], M["zt", gi])
                if gi >= 1:
                    t.wait_ge(P.sems["sV"], M["d2r", gi - 1])
                M["D2", gi] = P.inc(t.matmul(pGN[0:2, 256:384], onesblkb[:, :],
                                             zt[:, s2_:s2_ + 128],
                                             start=True, stop=True), "sP")
                if gi >= 1:
                    t.wait_ge(P.sems["sV"], M["state", gi - 1])
                M["dS", gi] = P.inc(t.matmul(pDS[:, :], khT[:, s2_:s2_ + 128],
                                             v2T[:, s2_:s2_ + 128],
                                             start=True, stop=True), "sP")

            @P.sec("vector")
            def _(v):
                v.wait_ge(P.sems["sP"], M["D2", gi])
                if gi >= 1:
                    v.wait_ge(P.sems["sP"], M["pDb", gi - 1])
                M["d2r", gi] = P.inc(v.tensor_copy(d2row[:, :],
                                                   pGN[0:2, 256:384]), "sV")

            @P.sec("tensor")
            def _(t):
                t.wait_ge(P.sems["sV"], M["d2r", gi])
                if gi >= 1:
                    t.wait_ge(P.sems["sV"], M["ysq1", gi - 1])
                M["pDb", gi] = P.inc(t.matmul(pM[:, 0:128], ind2b[:, :],
                                              d2row[:, :],
                                              start=True, stop=True), "sP")

            @P.sec("vector")
            def _(v):
                v.wait_ge(P.sems["sA"], M["exp", gi])
                v.wait_ge(P.sems["sP"], M["dS", gi])
                v.wait_ge(P.sems["sV"], M["sbf", gi])
                m = P.inc(v.tensor_scalar(sblk2[:, nsl:nsl + 128],
                                          sblk2[:, s2_:s2_ + 128],
                                          fcol[:, gi % 2:gi % 2 + 1],
                                          None, ALU.mult), "sV")
                v.wait_ge(P.sems["sV"], m)
                P.inc(v.tensor_tensor(sblk2[0:64, nsl:nsl + 64],
                                      sblk2[0:64, nsl:nsl + 64],
                                      pDS[0:64, 0:64], ALU.add), "sV")
                M["state", gi] = P.inc(v.tensor_tensor(
                    sblk2[64:128, nsl + 64:nsl + 128],
                    sblk2[64:128, nsl + 64:nsl + 128],
                    pDS[64:128, 64:128], ALU.add), "sV")
                v.wait_ge(P.sems["sV"], M["state", gi])
                if ch_ == NCH - 1:
                    v.wait_ge(P.sems["sP"], M["Y", gi])
                    P.inc(v.memset(sblk2[:, :], 0.0), "sV")
                    M["sbf", gi + 1] = P.inc(v.memset(sbf2[:, :], 0.0), "sV")
                else:
                    M["sbf", gi + 1] = P.inc(v.tensor_copy(
                        sbf2[:, nsl:nsl + 128], sblk2[:, nsl:nsl + 128]), "sV")

            @P.sec("vector")
            def _(v):
                v.wait_ge(P.sems["sP"], M["pDb", gi])
                if gi >= 1:
                    v.wait_ge(P.sems["sP"], M["cs", gi - 1])
                M["ysq1", gi] = P.inc(v.tensor_tensor(ysq[:, :], pM[:, 0:128],
                                                      v2[:, l3:l3 + 128],
                                                      ALU.mult), "sV")
                v.wait_ge(P.sems["sP"], M["Y", gi])
                v.wait_ge(P.sems["sV"], M["ysq1", gi])
                M["ysb", gi] = P.inc(v.tensor_tensor(ysb[:, :], pY[:, :],
                                                     ysq[:, :], ALU.add), "sV")

            @P.sec("vector")
            def _(v):
                v.wait_ge(P.sems["sV"], M["ysb", gi])
                M["sq", gi] = P.inc(v.tensor_tensor(ysq[:, :], ysb[:, :],
                                                    ysb[:, :], ALU.mult), "sV")

            @P.sec("tensor")
            def _(t):
                t.wait_ge(P.sems["sV"], M["sq", gi])
                if gi >= 1:
                    t.wait_ge(P.sems["sV"], M["stats", gi - 1])
                t.matmul(pGN[0:2, 0:128], onesblk[:, :], ysb[:, :],
                         start=True, stop=True)
                M["cs", gi] = P.inc(t.matmul(pGN[0:2, 128:256], onesblk[:, :],
                                             ysq[:, :], start=True, stop=True),
                                    "sP")

            @P.sec("vector")
            def _(v):
                v.wait_ge(P.sems["sP"], M["cs", gi])
                if gi >= 1:
                    v.wait_ge(P.sems["sA"], M["sqrt", gi - 1])
                P.inc(v.tensor_scalar_mul(gnst[:, 0:128], pGN[0:2, 0:128],
                                          1.0 / 64), "sV")
                m = P.inc(v.tensor_scalar_mul(gnst[:, 128:256],
                                              pGN[0:2, 128:256], 1.0 / 64),
                          "sV")
                v.wait_ge(P.sems["sV"], m)
                m = P.inc(v.tensor_tensor(gnst[:, 256:384], gnst[:, 0:128],
                                          gnst[:, 0:128], ALU.mult), "sV")
                v.wait_ge(P.sems["sV"], m)
                m = P.inc(v.tensor_tensor(gnst[:, 128:256], gnst[:, 128:256],
                                          gnst[:, 256:384], ALU.subtract), "sV")
                v.wait_ge(P.sems["sV"], m)
                M["stats", gi] = P.inc(v.tensor_scalar_add(gnst[:, 128:256],
                                                           gnst[:, 128:256],
                                                           16e-5), "sV")

            @P.sec("scalar")
            def _(sc):
                sc.wait_ge(P.sems["sV"], M["stats", gi])
                if gi >= 1:
                    sc.wait_ge(P.sems["sV"], M["neg", gi - 1])
                m = P.inc(sc.activation(gnst[:, 256:384],
                                        gnst[:, 128:256], AF.Ln), "sA")
                sc.wait_ge(P.sems["sA"], m)
                M["sqrt", gi] = P.inc(sc.activation(gnst[:, 384:512],
                                                    gnst[:, 256:384], AF.Exp,
                                                    scale=-0.5), "sA")

            @P.sec("vector")
            def _(v):
                v.wait_ge(P.sems["sA"], M["sqrt", gi])
                M["neg", gi] = P.inc(v.scalar_tensor_tensor(
                    gnst[:, 512:640], gnst[:, 0:128], -1.0,
                    gnst[:, 384:512], ALU.mult, ALU.mult), "sV")

            @P.sec("tensor")
            def _(t):
                t.wait_ge(P.sems["sV"], M["neg", gi])
                if gi >= 1:
                    t.wait_ge(P.sems["sV"], M["apply", gi - 1])
                t.matmul(pM[:, 128:256], ind2[:, :], gnst[:, 384:512],
                         start=True, stop=True)
                M["pB", gi] = P.inc(t.matmul(pM[:, 256:384], ind2[:, :],
                                             gnst[:, 512:640],
                                             start=True, stop=True), "sP")

            @P.sec("vector")
            def _(v):
                v.wait_ge(P.sems["sP"], M["pB", gi])
                m = P.inc(v.tensor_tensor(ysq[:, :], ysb[:, :], pM[:, 128:256],
                                          ALU.mult), "sV")
                v.wait_ge(P.sems["sV"], m)
                M["apply", gi] = P.inc(v.tensor_tensor(ysq[:, :], ysq[:, :],
                                                       pM[:, 256:384],
                                                       ALU.add), "sV")
                v.wait_ge(P.sems["sV"], M["apply", gi])
                M["gb", gi] = P.inc(v.tensor_scalar(ysb[:, :], ysq[:, :],
                                                    scol(C_GAM, hp_),
                                                    scol(C_BET, hp_),
                                                    ALU.mult, ALU.add), "sV")
                v.wait_ge(P.sems["sV"], M["gb", gi])
                M["final", gi] = P.inc(v.tensor_tensor(
                    gybuf[:, hp_ * TF + ch_ * 128:hp_ * TF + (ch_ + 1) * 128],
                    ysb[:, :], g2[:, l3:l3 + 128], ALU.mult), "sV")

            if gi + 3 < NGI:
                emit_ld(gi + 3)

        emit_ld(0)
        emit_ld(1)
        emit_ld(2)
        for gi in range(NGI):
            emit_H(gi)
            if gi >= 1:
                emit_T(gi - 1)
        emit_T(NGI - 1)
        P.barrier()

    _wsc.__exit__(None, None, None)
    _wsd = nc.named_scope(pfx + "phD"); _wsd.__enter__()
    with contextlib.ExitStack() as pstk:
        pA = pstk.enter_context(nc.psum_tensor(pfx + "pA2", [128, 512], F32))
        pB = pstk.enter_context(nc.psum_tensor(pfx + "pB2_", [128, 512], F32))
        pC = pstk.enter_context(nc.psum_tensor(pfx + "pC2", [128, 512], F32))
        pD = pstk.enter_context(nc.psum_tensor(pfx + "pD2", [128, 512], F32))

        def epi_part(o, ti, tw, pair, spmark, i, sv0_):
            col0, width = tw
            slot = i % 2
            stname = "stA" if slot == 0 else "stB"
            st_slot_prev = P.cnt[stname]

            @P.sec("vector")
            def _(v):
                v.wait_ge(P.sems["sP"], spmark)
                if st_slot_prev > 0:
                    v.wait_ge(P.sems[stname], st_slot_prev)
                P.inc(v.tensor_copy(yout[:, slot * TTW:slot * TTW + width],
                                    pair[0][:, 0:width]), "sV")

            @P.sec("sync")
            def _(sy):
                P.wait(sy, "sV")
                P.inc(sy.dma_start(y[o * 128:(o + 1) * 128, col0:col0 + width],
                                   yout[:, slot * TTW:slot * TTW + width]),
                      stname, 16)

        emit_mm_stage(P, (wbufA, wbufB), [(pA, pB), (pC, pD)], [wop], 8, 4,
                      gybuf, TF, [(i * 512, 512) for i in range(8)], epi_part, 1)

    _wsd.__exit__(None, None, None)

    @P.sec("sync")
    def _(sy):
        P.wait(sy, "stA")
        P.wait(sy, "stB")
    P.pop_scope()




def build_wkv():
    P = Prog()
    io = _io_wkv(P.nc)
    P.start_block()
    emit_wkv(P, io)
    P.close()
    return P.nc


def _io_mamba_fin(nc, pfx="", internal=False, act_internal=False):
    return {
        "xT": nc.dram_tensor(pfx + "xT", [D, TOK], F32, kind="ExternalInput" if not act_internal else "Internal"),
        "p0": nc.dram_tensor(pfx + "p0", [D, TOK], F32, kind="ExternalInput" if not act_internal else "Internal"),
        "p1": nc.dram_tensor(pfx + "p1", [D, TOK], F32, kind="ExternalInput" if not act_internal else "Internal"),
        "lnwd": nc.dram_tensor(pfx + "lnwd", [128, 8], F32, kind="ExternalInput" if not internal else "Internal"),
        "lnbd": nc.dram_tensor(pfx + "lnbd", [128, 8], F32, kind="ExternalInput" if not internal else "Internal"),
        "y": nc.dram_tensor(pfx + "y", [D, TOK], F32, kind="ExternalOutput" if not act_internal else "Internal"),
    }


def emit_mamba_fin(P, io, pfx=""):
    nc = P.nc
    P.push_scope(pfx)
    xT = io["xT"]
    p0 = io["p0"]
    p1 = io["p1"]
    lnwd = io["lnwd"]
    lnbd = io["lnbd"]
    y = io["y"]

    TTW = 512
    xbuf = P.sb("xbuf", [128, 8 * TTW], F32)
    xhat = P.sb("xhat", [128, 8 * TTW], BF16)
    hbuf = P.sb("hbuf", [128, 8 * TTW], F32)
    scratch = P.sb("scratch", [128, 2048], F32)
    smallb = P.sb("smallb", [128, 16], F32)
    stat = P.sb("stat", [1, 4 * TTW], F32)
    ones_col = P.sb("ones_col", [128, 1], F32)
    ones_row = P.sb("ones_row", [1, 128], F32)
    yout = P.sb("yout", [128, 8 * TTW], F32)

    pA = P.ps("pA", [128, 512])
    pB = P.ps("pB", [128, 512])
    pS1 = P.ps("pS1", [1, 512])
    pS2 = P.ps("pS2", [1, 512])

    P.sem("ldx", "ldc", "ldp", "sP", "sV", "sA", "stA", "lnfree")

    @P.sec("sync")
    def _(sy):
        P.inc(sy.dma_start(smallb[:, 0:8], lnwd[:, :]), "ldc", 16)
        P.inc(sy.dma_start(smallb[:, 8:16], lnbd[:, :]), "ldc", 16)

    @P.sec("vector")
    def _(v):
        v.memset(ones_col[:, :], 1.0)
        P.inc(v.memset(ones_row[:, :], 1.0), "lnfree")
    P.barrier()

    for tt in range(TOK // TTW):
        emit_layernorm(P, xsrc_dram=xT, tcol0=tt * TTW, tcols=TTW,
                       xbuf=xbuf, xhat=xhat, xh_col0=0, scratch=scratch,
                       pS1=pS1, pS2=pS2, pBC1=pA, pBC2=pB,
                       ones_col=ones_col, ones_row=ones_row, stat=stat)

        @P.sec("gpsimd")
        def _(g):
            P.wait(g, "sV")
            for c in range(8):
                P.inc(g.dma_start(hbuf[:, c * TTW:(c + 1) * TTW],
                                  p0[c * 128:(c + 1) * 128,
                                     tt * TTW:(tt + 1) * TTW]), "ldp", 16)

        @P.sec("sync")
        def _(sy):
            P.wait(sy, "sV")
            P.wait(sy, "stA")
            for c in range(8):
                P.inc(sy.dma_start(yout[:, c * TTW:(c + 1) * TTW],
                                   p1[c * 128:(c + 1) * 128,
                                      tt * TTW:(tt + 1) * TTW]), "ldx", 16)

        @P.sec("vector")
        def _(v):
            P.wait(v, "ldp")
            P.wait(v, "ldx")
            P.wait(v, "sV")
            for c in range(8):
                P.inc(v.tensor_tensor(hbuf[:, c * TTW:(c + 1) * TTW],
                                      hbuf[:, c * TTW:(c + 1) * TTW],
                                      yout[:, c * TTW:(c + 1) * TTW], ALU.add),
                      "sV")
                P.wait(v, "sV")
                P.inc(v.tensor_tensor(hbuf[:, c * TTW:(c + 1) * TTW],
                                      hbuf[:, c * TTW:(c + 1) * TTW],
                                      xhat[:, c * TTW:(c + 1) * TTW], ALU.add),
                      "sV")
                P.wait(v, "sV")

        sp0 = P.cnt["sP"]

        @P.sec("scalar")
        def _(sc):
            P.wait(sc, "sV")
            for j in range(4):
                ins = sc.activation(scratch[:, j * TTW:(j + 1) * TTW],
                                    hbuf[:, j * TTW:(j + 1) * TTW], AF.Square)
            P.inc(ins, "sA")

        @P.sec("tensor")
        def _(t):
            P.wait(t, "sV")
            for j in range(4):
                t.matmul(pS1[0:1, 0:TTW], ones_col[:, 0:1],
                         hbuf[:, j * TTW:(j + 1) * TTW],
                         start=(j == 0), stop=False)
            P.wait(t, "sA")
            for j in range(4):
                ins = t.matmul(pS2[0:1, 0:TTW], ones_col[:, 0:1],
                               scratch[:, j * TTW:(j + 1) * TTW],
                               start=(j == 0), stop=False)
            P.inc(ins, "sP")

        @P.sec("scalar")
        def _(sc):
            P.wait(sc, "sP")
            for j in range(4, 8):
                ins = sc.activation(scratch[:, (j - 4) * TTW:(j - 3) * TTW],
                                    hbuf[:, j * TTW:(j + 1) * TTW], AF.Square)
            P.inc(ins, "sA")

        @P.sec("tensor")
        def _(t):
            for j in range(4, 8):
                t.matmul(pS1[0:1, 0:TTW], ones_col[:, 0:1],
                         hbuf[:, j * TTW:(j + 1) * TTW],
                         start=False, stop=(j == 7))
            P.wait(t, "sA")
            for j in range(4, 8):
                ins = t.matmul(pS2[0:1, 0:TTW], ones_col[:, 0:1],
                               scratch[:, (j - 4) * TTW:(j - 3) * TTW],
                               start=False, stop=(j == 7))
            P.inc(ins, "sP")

        mean = stat[0:1, 0:TTW]
        var = stat[0:1, TTW:2 * TTW]
        rstd = stat[0:1, 2 * TTW:3 * TTW]
        nmr = stat[0:1, 3 * TTW:4 * TTW]

        @P.sec("vector")
        def _(v):
            P.wait(v, "sP")
            v.tensor_scalar_mul(mean, pS1[0:1, 0:TTW], 1.0 / D)
            P.inc(v.tensor_scalar_mul(var, pS2[0:1, 0:TTW], 1.0 / D), "sV")
            P.wait(v, "sV")
            P.inc(v.tensor_tensor(scratch[0:1, 0:TTW], mean, mean, ALU.mult), "sV")
            P.wait(v, "sV")
            P.inc(v.tensor_tensor(var, var, scratch[0:1, 0:TTW], ALU.subtract),
                  "sV")
            P.wait(v, "sV")
            P.inc(v.tensor_scalar_add(var, var, 1e-5), "sV")

        @P.sec("scalar")
        def _(sc):
            P.wait(sc, "sV")
            P.inc(sc.activation(scratch[0:1, 0:TTW], var, AF.Sqrt), "sA")

        @P.sec("vector")
        def _(v):
            P.wait(v, "sA")
            P.inc(v.reciprocal(rstd, scratch[0:1, 0:TTW]), "sV")
            P.wait(v, "sV")
            P.inc(v.tensor_tensor(nmr, mean, rstd, ALU.mult), "sV")
            P.wait(v, "sV")
            P.inc(v.tensor_scalar_mul(nmr, nmr, -1.0), "sV")

        @P.sec("tensor")
        def _(t):
            P.wait(t, "sV")
            t.matmul(pA[:, 0:TTW], ones_row[0:1, 0:128], rstd,
                     start=True, stop=True)
            ins = t.matmul(pB[:, 0:TTW], ones_row[0:1, 0:128], nmr,
                           start=True, stop=True)
            P.inc(ins, "sP")

        for c in range(8):
            @P.sec("vector")
            def _(v):
                P.wait(v, "sP")
                if c == 0:
                    P.wait(v, "stA")
                P.inc(v.tensor_tensor(scratch[:, 0:TTW],
                                      hbuf[:, c * TTW:(c + 1) * TTW],
                                      pA[:, 0:TTW], ALU.mult), "sV")
                P.wait(v, "sV")
                P.inc(v.tensor_tensor(scratch[:, 0:TTW], scratch[:, 0:TTW],
                                      pB[:, 0:TTW], ALU.add), "sV")

            @P.sec("vector")
            def _(v):
                P.wait(v, "sV")
                P.wait(v, "ldc")
                P.inc(v.tensor_scalar(scratch[:, TTW:2 * TTW], scratch[:, 0:TTW],
                                      smallb[:, c:c + 1], smallb[:, 8 + c:9 + c],
                                      ALU.mult, ALU.add), "sV")
                P.wait(v, "sV")
                ins = v.tensor_tensor(yout[:, c * TTW:(c + 1) * TTW],
                                      scratch[:, TTW:2 * TTW],
                                      xbuf[:, c * TTW:(c + 1) * TTW], ALU.add)
                P.inc(ins, "sV")

            @P.sec("sync")
            def _(sy):
                P.wait(sy, "sV")
                P.inc(sy.dma_start(y[c * 128:(c + 1) * 128,
                                     tt * TTW:(tt + 1) * TTW],
                                   yout[:, c * TTW:(c + 1) * TTW]), "stA", 16)
        P.barrier()

    @P.sec("sync")
    def _(sy):
        P.wait(sy, "stA")
    P.pop_scope()



def build_mamba_fin():
    P = Prog()
    io = _io_mamba_fin(P.nc)
    P.start_block()
    emit_mamba_fin(P, io)
    P.close()
    return P.nc


_NC_CACHE = {}


def _ffn_prep(wd, pref):
    return {
        "w1p": _prep_w(wd[pref + "w1_w"]), "w2p": _prep_w(wd[pref + "w2_w"]),
        "h1p": _prep_w(wd[pref + "h1_w"]), "h2p": _prep_w(wd[pref + "h2_w"]),
        "w3p": _prep_w(wd[pref + "w3_w"]),
        "s1": _prep_s(wd[pref + "w1_s"]), "s2": _prep_s(wd[pref + "w2_s"]),
        "sh1": _prep_s(wd[pref + "h1_s"]), "sh2": _prep_s(wd[pref + "h2_s"]),
        "s3": _prep_s(wd[pref + "w3_s"]),
    }


def run_ffn(xT_cores, prep):
    if "ffn" not in _NC_CACHE:
        _NC_CACHE["ffn"] = build_ffn()
    in_maps = [{"xT": xT_cores[c], **prep} for c in range(NCORES)]
    res = run_bass_kernel_spmd(_NC_CACHE["ffn"], in_maps, core_ids=list(range(NCORES)))
    return [res.results[c]["y"] for c in range(NCORES)]


GROUPS2 = [[0, 1], [2, 3], [4, 5], [6, 7]]


def _gate(P, name):
    for e in ("tensor", "vector", "scalar", "sync", "gpsimd"):
        @P.sec(e)
        def _(eng):
            P.wait(eng, name)


def build_full():
    P = Prog()
    nc = P.nc
    xT = nc.dram_tensor("xT", [D, TOK], F32, kind="ExternalInput")
    yT = nc.dram_tensor("yT", [D, TOK], F32, kind="ExternalOutput")

    x0i = nc.dram_tensor("x0i", [D, TOK], F32, kind="Internal")
    g0 = nc.dram_tensor("g0", [2 * D, TOK], F32, kind="Internal")
    pa_l = nc.dram_tensor("pa_l", [2 * D, TOK], F32, kind="Internal")
    pa_r = nc.dram_tensor("pa_r", [D, TOK], F32, kind="Internal")
    x1i = nc.dram_tensor("x1i", [D, TOK], F32, kind="Internal")
    x2i = nc.dram_tensor("x2i", [D, TOK], F32, kind="Internal")
    gh = nc.dram_tensor("gh", [2 * D, 128], F32, kind="Internal")
    hsrc = nc.dram_tensor("hsrc", [D, 128], F32, kind="Internal")
    xei = nc.dram_tensor("xei", [D, TOK + 128], F32, kind="Internal")
    x3i = nc.dram_tensor("x3i", [D, TOK], F32, kind="Internal")
    x4i = nc.dram_tensor("x4i", [D, TOK], F32, kind="Internal")
    g4 = nc.dram_tensor("g4", [2 * D, TOK], F32, kind="Internal")
    pw_l = nc.dram_tensor("pw_l", [2 * D, TOK], F32, kind="Internal")
    pw_r = nc.dram_tensor("pw_r", [D, TOK], F32, kind="Internal")
    x5i = nc.dram_tensor("x5i", [D, TOK], F32, kind="Internal")
    x6i = nc.dram_tensor("x6i", [D, TOK], F32, kind="Internal")
    g6 = nc.dram_tensor("g6", [2 * D, TOK], F32, kind="Internal")
    pm_l = nc.dram_tensor("pm_l", [2 * D, TOK], F32, kind="Internal")
    pm_r = nc.dram_tensor("pm_r", [D, TOK], F32, kind="Internal")
    x7i = nc.dram_tensor("x7i", [D, TOK], F32, kind="Internal")
    z8 = nc.dram_tensor("z8", [D, TOK], F32, kind="Internal")

    class _HalvedView:

        def __init__(self, t2d):
            self.t = t2d

        def __getitem__(self, idx):
            rs, cs = idx
            c0 = cs.start or 0
            c1 = cs.stop
            h = c0 // TOK
            assert (c1 - 1) // TOK == h, (c0, c1)
            r0 = rs.start or 0
            r1 = rs.stop
            return self.t[h * D + r0:h * D + r1, c0 - h * TOK:c1 - h * TOK]

    def fullview(t2d):
        return _HalvedView(t2d)

    io_aa = _io_attn_a(nc, pfx="aa_", act_internal=True)
    io_aa["xTf"] = fullview(g0)
    io_aa["y"] = fullview(pa_l)
    io_g1 = _io_merge(nc, pfx="g1_", act_internal=True)
    io_g1.update(x0=x0i.ap(), x1=pa_r.ap(), x2=z8.ap(), y=x1i.ap())
    io_f = _io_ffn(nc, pfx="f_", act_internal=True)
    io_f1 = dict(io_f); io_f1.update(xT=x1i.ap(), y=x2i.ap())
    io_ab = _io_attn_b(nc, pfx="ab_", act_internal=True)
    io_ab.update(xTe=xei.ap(), y=x3i.ap())
    io_f2 = dict(io_f); io_f2.update(xT=x3i.ap(), y=x4i.ap())
    io_wk = _io_wkv(nc, pfx="wk_", act_internal=True)
    io_wk["xTf"] = fullview(g4)
    io_wk["y"] = fullview(pw_l)
    io_g2 = _io_merge(nc, pfx="g2_", act_internal=True)
    io_g2.update(x0=x4i.ap(), x1=pw_r.ap(), x2=z8.ap(), y=x5i.ap())
    io_mo = _io_moe(nc, pfx="mo_", act_internal=True)
    io_mo1 = dict(io_mo); io_mo1.update(xT=x5i.ap(), y=x6i.ap())
    io_ma = _io_mamba(nc, pfx="ma_", act_internal=True)
    io_ma["xTf"] = fullview(g6)
    io_ma["y"] = fullview(pm_l)
    io_fi = _io_mamba_fin(nc, pfx="fi_", act_internal=True)
    io_fi.update(xT=x6i.ap(), p0=pm_r.ap(), p1=z8.ap(), y=x7i.ap())
    io_mo2 = dict(io_mo); io_mo2.update(xT=x7i.ap(), y=yT.ap())

    P.sem("ccs", "stF")
    P.start_block()

    P.push_scope("pro_")
    zb = P.sb("zb", [128, 2048], F32)

    @P.sec("vector")
    def _(v):
        P.inc(v.memset(zb[:, :], 0.0), "stF")

    @P.sec("sync")
    def _(sy):
        P.wait(sy, "stF")
        for k in range(8):
            P.inc(sy.dma_start(z8[k * 128:(k + 1) * 128, :],
                               zb[:, :]), "stF", 16)
        P.inc(sy.dma_start(x0i[:, :], xT[:, :]), "stF", 16)
        P.wait(sy, "stF")
    P.pop_scope()
    P.barrier()

    def collective(kind, src_ap, dst, alu=ALU.add):
        @P.sec("gpsimd")
        def _(g):
            P.wait(g, "bar")
            P.inc(g.collective_compute(kind, alu, replica_groups=GROUPS2,
                                       ins=[src_ap], outs=[dst.ap().opt()]),
                  "ccs", 1)
        _gate(P, "ccs")

    collective("AllGather", x0i.ap().opt(), g0, ALU.bypass)
    with nc.named_scope("attn_a"):
        emit_attn_a(P, io_aa, pfx="aa_")
    P.barrier()
    collective("ReduceScatter", pa_l.ap().opt(), pa_r, ALU.add)
    emit_merge(P, io_g1, pfx="g1_")
    P.barrier()
    with nc.named_scope("ffn1"):
        emit_ffn(P, io_f1, pfx="f1_")
    P.barrier()

    @P.sec("sync")
    def _(sy):
        P.wait(sy, "bar")
        P.inc(sy.dma_start(hsrc[:, :], x2i[:, TOK - 128:TOK]), "stF", 16)
        P.wait(sy, "stF")
        P.inc(sy.drain(), "ccs")
    _gate(P, "ccs")
    collective("AllGather", hsrc.ap().opt(), gh, ALU.bypass)

    @P.sec("sync")
    def _(sy):
        P.wait(sy, "ccs")
        P.inc(sy.dma_start(xei[:, 0:128], gh[0:D, :]), "stF", 16)
        P.inc(sy.dma_start(xei[:, 128:TOK + 128], x2i[:, :]), "stF", 16)
        P.wait(sy, "stF")
    P.barrier()
    with nc.named_scope("attn_b"):
        emit_attn_b(P, io_ab, pfx="ab_")
    P.barrier()
    with nc.named_scope("ffn2"):
        emit_ffn(P, io_f2, pfx="f2_")
    P.barrier()

    collective("AllGather", x4i.ap().opt(), g4, ALU.bypass)
    with nc.named_scope("wkv"):
        emit_wkv(P, io_wk, pfx="wk_")
    P.barrier()
    collective("ReduceScatter", pw_l.ap().opt(), pw_r, ALU.add)
    emit_merge(P, io_g2, pfx="g2_")
    P.barrier()
    with nc.named_scope("moe1"):
        emit_moe(P, io_mo1, pfx="mo1_")
    P.barrier()

    collective("AllGather", x6i.ap().opt(), g6, ALU.bypass)
    with nc.named_scope("mamba"):
        emit_mamba(P, io_ma, pfx="ma_")
    P.barrier()
    collective("ReduceScatter", pm_l.ap().opt(), pm_r, ALU.add)
    with nc.named_scope("fin"):
        emit_mamba_fin(P, io_fi, pfx="fi_")
    P.barrier()
    with nc.named_scope("moe2"):
        emit_moe(P, io_mo2, pfx="mo2_")

    P.close()
    return P.nc


def _launch(name, builder, in_maps):
    if name not in _NC_CACHE:
        _NC_CACHE[name] = builder()
    res = run_bass_kernel_spmd(_NC_CACHE[name], in_maps, core_ids=list(range(NCORES)))
    return [res.results[c]["y"] for c in range(NCORES)]


def _full_from_halves(parts):
    return [np.concatenate([parts[2 * b], parts[2 * b + 1]], axis=1)
            for b in range(B)]


def _pad_cols(w, cols):
    out = np.zeros((w.shape[0], cols), np.float32)
    out[:, :w.shape[1]] = np.asarray(w, np.float32)
    return out


def kernel_unfused(**inputs):
    g = {k: np.asarray(v, np.float32) for k, v in inputs.items()}
    x0 = _shard_xT(g["L0_attn_a_x"])

    inv_freq = 1.0 / 10000.0 ** (np.arange(0, 64, 2) / 64)
    fr = np.repeat(np.arange(S)[:, None] * inv_freq[None, :], 2, axis=-1)
    cosd = np.zeros((128, S), np.float32)
    sind = np.zeros((128, S), np.float32)
    ct, st_ = np.cos(fr), np.sin(fr)
    for r_ in range(128):
        d_ = r_ % 64
        cosd[r_] = ct[:, d_]
        sind[r_] = st_[:, d_] * (-1.0 if d_ % 2 == 0 else 1.0)
    aa_maps = []
    aa_prep = {}
    for hh in range(2):
        sl = slice(hh * 512, hh * 512 + 512)
        pm = np.arange(512) ^ 1
        aa_prep[hh] = {
            "wqp": _prep_w(g["L0_attn_a_Wq"][sl]),
            "wqpp": _prep_w(g["L0_attn_a_Wq"][sl][pm]),
            "wkp": _prep_w(g["L0_attn_a_Wk"][sl]),
            "wkpp": _prep_w(g["L0_attn_a_Wk"][sl][pm]),
            "wvT": np.ascontiguousarray(g["L0_attn_a_Wv"][sl].T),
            "wop": _prep_w(g["L0_attn_a_Wo"][:, sl]),
            "bq": _prep_s(g["L0_attn_a_bq"][sl]),
            "bqp": _prep_s(g["L0_attn_a_bq"][sl][pm]),
            "bk": _prep_s(g["L0_attn_a_bk"][sl]),
            "bkp": _prep_s(g["L0_attn_a_bk"][sl][pm]),
            "bv": np.ascontiguousarray(g["L0_attn_a_bv"][sl][None, :]),
            "cosd": cosd, "sind": sind,
        }
    xf0 = _full_from_halves(x0)
    for c in range(NCORES):
        aa_maps.append({"xTf": xf0[c // 2], **aa_prep[c % 2]})
    pa = _launch("attn_a", build_attn_a, aa_maps)

    bo_p = _prep_s(g["L0_attn_a_bo"])
    zeros_b = np.zeros((128, 8), np.float32)
    mg = []
    for c in range(NCORES):
        b_, h_ = c // 2, c % 2
        cs = slice(h_ * TOK, (h_ + 1) * TOK)
        mg.append({"x0": x0[c], "x1": np.ascontiguousarray(pa[2 * b_][:, cs]),
                   "x2": np.ascontiguousarray(pa[2 * b_ + 1][:, cs]), "bd": bo_p})
    x1 = _launch("merge", build_merge, mg)

    fprep = _ffn_prep(g, "L0_ffn_")
    x2 = _launch("ffn", build_ffn, [{"xT": x1[c], **fprep} for c in range(NCORES)])

    mrest = np.full((128, 256), -1e30, np.float32)
    for qi in range(128):
        mrest[qi, qi + 120:qi + 129] = 0.0
    mfirst = np.where(np.arange(256)[None, :] >= 128, mrest, -1e30).astype(np.float32)
    ab_prep = {
        "wqp": _prep_w(g["L0_attn_b_Wq"].T), "wkp": _prep_w(g["L0_attn_b_Wk"].T),
        "wvT": np.ascontiguousarray(g["L0_attn_b_Wv"]),
        "wop": _prep_w(g["L0_attn_b_Wo"].T), "mrest": mrest,
    }
    ab = []
    for c in range(NCORES):
        b_, h_ = c // 2, c % 2
        xe = np.zeros((D, TOK + 128), np.float32)
        xe[:, 128:] = x2[c]
        if h_ == 1:
            xe[:, :128] = x2[2 * b_][:, TOK - 128:]
        ab.append({"xTe": xe, **ab_prep,
                   "mfirst": (mfirst if h_ == 0 else mrest)})
    x3 = _launch("attn_b", build_attn_b, ab)

    x4 = _launch("ffn", build_ffn, [{"xT": x3[c], **fprep} for c in range(NCORES)])

    mask = np.zeros((16, 2048), np.float32)
    for sl_ in range(16):
        for q_ in range(16):
            t0 = 8 * q_ + (sl_ % 8) + 1
            mask[sl_, q_ * 128 + t0:(q_ + 1) * 128] = 1.0
    ind2d = np.zeros((2, 128), np.float32)
    ind2d[0, :64] = 1.0
    ind2d[1, 64:] = 1.0
    wk_prep = {}
    for hh in range(2):
        sl = slice(hh * 512, hh * 512 + 512)
        wk_prep[hh] = {
            "ap": _prep_w(_pad_cols(g["L1_attn_a_A"], 256).T),
            "bpp": np.ascontiguousarray(
                g["L1_attn_a_Bp"].reshape(5, 32, 8, 128).transpose(0, 2, 1, 3)),
            "tdap": _prep_w(_pad_cols(g["L1_attn_a_tdA"], 128).T),
            "tdbp": _prep_w(_pad_cols(g["L1_attn_a_tdB"].T[sl], 128)),
            "wg1p": _prep_w(_pad_cols(g["L1_attn_a_Wg1"], 256).T),
            "wg2p": _prep_w(_pad_cols(g["L1_attn_a_Wg2"].T[sl], 256)),
            "wkp": _prep_w(g["L1_attn_a_Wk"][sl]),
            "wvp": _prep_w(g["L1_attn_a_Wv"][sl]),
            "wrp": _prep_w(g["L1_attn_a_Wr"][sl]),
            "wop": _prep_w(g["L1_attn_a_Wo"][:, sl]),
            "miud": _prep_s(g["L1_attn_a_miu_x"][0, 0]),
            "lamd": np.ascontiguousarray(np.concatenate(
                [_prep_s(g["L1_attn_a_lam"][s_, 0, 0]) for s_ in range(5)], axis=1)),
            "tdmd": _prep_s(g["L1_attn_a_td_miu"][0, 0][sl]),
            "ud": np.ascontiguousarray(
                g["L1_attn_a_u"][hh * 8:hh * 8 + 8].reshape(4, 128).T),
            "gamd": _prep_s(g["L1_attn_a_gamma"][sl]),
            "betd": _prep_s(g["L1_attn_a_beta"][sl]),
            "maskd": mask, "ind2d": ind2d,
        }
    xf4 = _full_from_halves(x4)
    pw = _launch("wkv", build_wkv,
                 [{"xTf": xf4[c // 2], **wk_prep[c % 2]} for c in range(NCORES)])

    mg = []
    for c in range(NCORES):
        b_, h_ = c // 2, c % 2
        cs = slice(h_ * TOK, (h_ + 1) * TOK)
        mg.append({"x0": x4[c], "x1": np.ascontiguousarray(pw[2 * b_][:, cs]),
                   "x2": np.ascontiguousarray(pw[2 * b_ + 1][:, cs]),
                   "bd": zeros_b})
    x5 = _launch("merge", build_merge, mg)

    E, R = 8, 64
    W1s = g["L1_ffn_W1"].reshape(E * R, D)
    W2s = np.ascontiguousarray(g["L1_ffn_W2"].transpose(1, 0, 2).reshape(D, E * R))
    ind = np.zeros((8, 512), np.float32)
    for mcol in range(512):
        ind[mcol // 64, mcol] = 1.0
    moe_prep = {
        "w1p": _prep_w(W1s), "w2p": _prep_w(W2s),
        "wgp": np.ascontiguousarray(
            g["L1_ffn_Wg"].T.reshape(8, 128, 8).transpose(1, 0, 2)),
        "bgd": np.ascontiguousarray(g["L1_ffn_bg"][None, :]), "indd": ind,
    }
    x6 = _launch("moe", build_moe, [{"xT": x5[c], **moe_prep} for c in range(NCORES)])

    DI = 2048
    mam_prep = {}
    for dsl in range(2):
        sl = slice(dsl * 1024, dsl * 1024 + 1024)
        other = slice((1 - dsl) * 1024, (1 - dsl) * 1024 + 1024)
        order = np.r_[np.arange(sl.start, sl.stop),
                      np.arange(other.start, other.stop)]
        mam_prep[dsl] = {
            "winxp": _prep_w(g["L1_attn_b_W_in"][:DI][order]),
            "wznp": _prep_w(g["L1_attn_b_W_in"][DI:][sl]),
            "wxp": np.ascontiguousarray(
                g["L1_attn_b_W_x"][:, order].T.reshape(16, 128, 96)
                .transpose(1, 0, 2)),
            "wdtp": np.ascontiguousarray(
                g["L1_attn_b_W_dt"][sl].T.reshape(64, 8, 128)),
            "bdtd": _prep_s(g["L1_attn_b_b_dt"][sl]),
            "convwd": np.ascontiguousarray(
                g["L1_attn_b_conv_w"][order, 0, :].reshape(16, 128, 4)
                .transpose(1, 0, 2)),
            "convbd": np.ascontiguousarray(
                g["L1_attn_b_conv_b"][order].reshape(16, 128).T),
            "acold": np.ascontiguousarray(
                g["L1_attn_b_A_log"][sl].reshape(8, 128, 16)
                .transpose(1, 0, 2).reshape(128, 128)),
            "dcold": _prep_s(g["L1_attn_b_D_param"][sl]),
            "woutp": _prep_w(g["L1_attn_b_W_out"][:, sl]),
        }
    xf6 = _full_from_halves(x6)
    pm_ = _launch("mamba", build_mamba,
                  [{"xTf": xf6[c // 2], **mam_prep[c % 2]} for c in range(NCORES)])

    fin_prep = {"lnwd": _prep_s(g["L1_attn_b_ln_w"]),
                "lnbd": _prep_s(g["L1_attn_b_ln_b"])}
    mg = []
    for c in range(NCORES):
        b_, h_ = c // 2, c % 2
        cs = slice(h_ * TOK, (h_ + 1) * TOK)
        mg.append({"xT": x6[c], "p0": np.ascontiguousarray(pm_[2 * b_][:, cs]),
                   "p1": np.ascontiguousarray(pm_[2 * b_ + 1][:, cs]), **fin_prep})
    x7 = _launch("mamba_fin", build_mamba_fin, mg)

    x8 = _launch("moe", build_moe, [{"xT": x7[c], **moe_prep} for c in range(NCORES)])

    return _unshard_xT(x8)


def _prep_all(g):
    inv_freq = 1.0 / 10000.0 ** (np.arange(0, 64, 2) / 64)
    fr = np.repeat(np.arange(S)[:, None] * inv_freq[None, :], 2, axis=-1)
    cosd = np.zeros((128, S), np.float32)
    sind = np.zeros((128, S), np.float32)
    ct, st_ = np.cos(fr), np.sin(fr)
    for r_ in range(128):
        d_ = r_ % 64
        cosd[r_] = ct[:, d_]
        sind[r_] = st_[:, d_] * (-1.0 if d_ % 2 == 0 else 1.0)
    shared = {}
    perhalf = [{}, {}]

    def sh(pfx, d):
        for k, v in d.items():
            shared[pfx + k] = v

    def ph(hh, pfx, d):
        for k, v in d.items():
            perhalf[hh][pfx + k] = v

    for hh in range(2):
        sl = slice(hh * 512, hh * 512 + 512)
        pm = np.arange(512) ^ 1
        ph(hh, "aa_", {
            "wqp": _prep_w(g["L0_attn_a_Wq"][sl]),
            "wqpp": _prep_w(g["L0_attn_a_Wq"][sl][pm]),
            "wkp": _prep_w(g["L0_attn_a_Wk"][sl]),
            "wkpp": _prep_w(g["L0_attn_a_Wk"][sl][pm]),
            "wvT": np.ascontiguousarray(g["L0_attn_a_Wv"][sl].T),
            "wop": _prep_w(g["L0_attn_a_Wo"][:, sl]),
            "bq": _prep_s(g["L0_attn_a_bq"][sl]),
            "bqp": _prep_s(g["L0_attn_a_bq"][sl][pm]),
            "bk": _prep_s(g["L0_attn_a_bk"][sl]),
            "bkp": _prep_s(g["L0_attn_a_bk"][sl][pm]),
            "bv": np.ascontiguousarray(g["L0_attn_a_bv"][sl][None, :]),
            "cosd": cosd, "sind": sind,
        })
    sh("g1_", {"bd": _prep_s(g["L0_attn_a_bo"])})
    sh("g2_", {"bd": np.zeros((128, 8), np.float32)})
    sh("f_", _ffn_prep(g, "L0_ffn_"))

    mrest = np.full((128, 256), -1e30, np.float32)
    for qi in range(128):
        mrest[qi, qi + 120:qi + 129] = 0.0
    mfirst = np.where(np.arange(256)[None, :] >= 128, mrest, -1e30).astype(np.float32)
    sh("ab_", {
        "wqp": _prep_w(g["L0_attn_b_Wq"].T), "wkp": _prep_w(g["L0_attn_b_Wk"].T),
        "wvT": np.ascontiguousarray(g["L0_attn_b_Wv"]),
        "wop": _prep_w(g["L0_attn_b_Wo"].T), "mrest": mrest,
    })
    for hh in range(2):
        ph(hh, "ab_", {"mfirst": (mfirst if hh == 0 else mrest)})

    mask = np.zeros((16, 2048), np.float32)
    for sl_ in range(16):
        for q_ in range(16):
            t0 = 8 * q_ + (sl_ % 8) + 1
            mask[sl_, q_ * 128 + t0:(q_ + 1) * 128] = 1.0
    ind2d = np.zeros((2, 128), np.float32)
    ind2d[0, :64] = 1.0
    ind2d[1, 64:] = 1.0
    for hh in range(2):
        sl = slice(hh * 512, hh * 512 + 512)
        ph(hh, "wk_", {
            "ap": _prep_w(_pad_cols(g["L1_attn_a_A"], 256).T),
            "bpp": np.ascontiguousarray(
                g["L1_attn_a_Bp"].reshape(5, 32, 8, 128).transpose(0, 2, 1, 3)),
            "tdap": _prep_w(_pad_cols(g["L1_attn_a_tdA"], 128).T),
            "tdbp": _prep_w(_pad_cols(g["L1_attn_a_tdB"].T[sl], 128)),
            "wg1p": _prep_w(_pad_cols(g["L1_attn_a_Wg1"], 256).T),
            "wg2p": _prep_w(_pad_cols(g["L1_attn_a_Wg2"].T[sl], 256)),
            "wkp": _prep_w(g["L1_attn_a_Wk"][sl]),
            "wvp": _prep_w(g["L1_attn_a_Wv"][sl]),
            "wrp": _prep_w(g["L1_attn_a_Wr"][sl]),
            "wop": _prep_w(g["L1_attn_a_Wo"][:, sl]),
            "miud": _prep_s(g["L1_attn_a_miu_x"][0, 0]),
            "lamd": np.ascontiguousarray(np.concatenate(
                [_prep_s(g["L1_attn_a_lam"][s_, 0, 0]) for s_ in range(5)], axis=1)),
            "tdmd": _prep_s(g["L1_attn_a_td_miu"][0, 0][sl]),
            "ud": np.ascontiguousarray(
                g["L1_attn_a_u"][hh * 8:hh * 8 + 8].reshape(4, 128).T),
            "gamd": _prep_s(g["L1_attn_a_gamma"][sl]),
            "betd": _prep_s(g["L1_attn_a_beta"][sl]),
            "maskd": mask, "ind2d": ind2d,
        })

    E, R = 8, 64
    W1s = g["L1_ffn_W1"].reshape(E * R, D)
    W2s = np.ascontiguousarray(g["L1_ffn_W2"].transpose(1, 0, 2).reshape(D, E * R))
    ind = np.zeros((8, 512), np.float32)
    for mcol in range(512):
        ind[mcol // 64, mcol] = 1.0
    sh("mo_", {
        "w1p": _prep_w(W1s), "w2p": _prep_w(W2s),
        "wgp": np.ascontiguousarray(
            g["L1_ffn_Wg"].T.reshape(8, 128, 8).transpose(1, 0, 2)),
        "bgd": np.ascontiguousarray(g["L1_ffn_bg"][None, :]), "indd": ind,
    })

    DI = 2048
    for dsl in range(2):
        sl = slice(dsl * 1024, dsl * 1024 + 1024)
        other = slice((1 - dsl) * 1024, (1 - dsl) * 1024 + 1024)
        order = np.r_[np.arange(sl.start, sl.stop),
                      np.arange(other.start, other.stop)]
        ph(dsl, "ma_", {
            "winxp": _prep_w(g["L1_attn_b_W_in"][:DI][order]),
            "wznp": _prep_w(g["L1_attn_b_W_in"][DI:][sl]),
            "wxp": np.ascontiguousarray(
                g["L1_attn_b_W_x"][:, order].T.reshape(16, 128, 96)
                .transpose(1, 0, 2)),
            "wdtp": np.ascontiguousarray(
                g["L1_attn_b_W_dt"][sl].T.reshape(64, 8, 128)),
            "bdtd": _prep_s(g["L1_attn_b_b_dt"][sl]),
            "convwd": np.ascontiguousarray(
                g["L1_attn_b_conv_w"][order, 0, :].reshape(16, 128, 4)
                .transpose(1, 0, 2)),
            "convbd": np.ascontiguousarray(
                g["L1_attn_b_conv_b"][order].reshape(16, 128).T),
            "acold": np.ascontiguousarray(
                g["L1_attn_b_A_log"][sl].reshape(8, 128, 16)
                .transpose(1, 0, 2).reshape(128, 128)),
            "dcold": _prep_s(g["L1_attn_b_D_param"][sl]),
            "woutp": _prep_w(g["L1_attn_b_W_out"][:, sl]),
        })
    sh("fi_", {"lnwd": _prep_s(g["L1_attn_b_ln_w"]),
               "lnbd": _prep_s(g["L1_attn_b_ln_b"])})
    return shared, perhalf


def kernel(**inputs):
    g = {k: np.asarray(v, np.float32) for k, v in inputs.items()}
    x0 = _shard_xT(g["L0_attn_a_x"])
    shared, perhalf = _prep_all(g)
    in_maps = [{"xT": x0[c], **shared, **perhalf[c % 2]} for c in range(NCORES)]
    if "full" not in _NC_CACHE:
        _NC_CACHE["full"] = build_full()
    res = run_bass_kernel_spmd(_NC_CACHE["full"], in_maps,
                               core_ids=list(range(NCORES)))
    return _unshard_xT([res.results[c]["yT"] for c in range(NCORES)])
